# version 02
# baseline (speedup 1.0000x reference)
import numpy as np
from contextlib import ExitStack
import concourse.bass as bass
import concourse.mybir as mybir
from concourse.bass_utils import run_bass_kernel_spmd

F32 = mybir.dt.float32
BF16 = mybir.dt.bfloat16
ALU = mybir.AluOpType
ACT = mybir.ActivationFunctionType
AX = mybir.AxisListType

USE_DRAIN = True
USE_BOOT = True
USE_EXIT = False
LOCKSTEP = True
MAGIC = 12582912.0
NCORES = 8
NSEQ = 4
HID = 1024
HEADS = 16
HDIM = 64
INNER = 4096
CH = 128
HB = 64
BETA = 0.9
EPS_LN = 1e-5
EPS_L2 = 1e-12
EPS_WQ = 1e-5
EPS_AQ = 1e-5


class P:
    def __init__(self):
        self.ops = {"sync": [], "gpsimd": [], "vector": [], "scalar": [], "tensor": []}
        self.cnt = {"sv": 1}

    def emit(self, eng, fn):
        self.ops[eng].append(fn)

    def inc(self, sem, n=1):
        self.cnt[sem] = self.cnt.get(sem, 0) + n
        return self.cnt[sem]

    def cur(self, sem):
        return self.cnt.get(sem, 0)


def build_program(T=2048, dbg=False):
    nc = bass.Bass("TRN2", target_bir_lowering=False, detect_race_conditions=False)
    NCH = T // CH

    x_in = nc.dram_tensor("x", [NSEQ * T, HID], F32, kind="ExternalInput")
    wq_d = nc.dram_tensor("wq", [HID, HID], F32, kind="ExternalInput")
    wk_d = nc.dram_tensor("wk", [HID, HID], F32, kind="ExternalInput")
    wv_d = nc.dram_tensor("wv", [HID, HID], F32, kind="ExternalInput")
    wo_d = nc.dram_tensor("wo", [HID, HID], F32, kind="ExternalInput")
    b_ds = [nc.dram_tensor(n, [HID], F32, kind="ExternalInput")
            for n in ("bq", "bk", "bv", "bo")]
    wg_d = nc.dram_tensor("wg", [HID, INNER], F32, kind="ExternalInput")
    wu_d = nc.dram_tensor("wu", [HID, INNER], F32, kind="ExternalInput")
    wd_d = nc.dram_tensor("wd", [INNER, HID], F32, kind="ExternalInput")
    y_out = nc.dram_tensor("y", [NSEQ * T, HID], F32, kind="ExternalOutput")

    wgq_d = nc.dram_tensor("wgq", [HID, INNER], BF16)
    wuq_d = nc.dram_tensor("wuq", [HID, INNER], BF16)
    wdq_d = nc.dram_tensor("wdq", [INNER, HID], BF16)
    qsc_d = nc.dram_tensor("qsc", [CH, NSEQ * HID], F32)
    ksc_d = nc.dram_tensor("ksc", [CH, NSEQ * HID], F32)
    vsc_d = nc.dram_tensor("vsc", [CH, NSEQ * HID], F32)
    osc_d = nc.dram_tensor("osc", [64, CH * HDIM], F32)
    msc_d = nc.dram_tensor("mscd", [8], F32)
    if dbg:
        dbg_q = nc.dram_tensor("dbg_q", [CH, NSEQ * HID], F32, kind="ExternalOutput")
        dbg_kp = nc.dram_tensor("dbg_kp", [64, HB * HDIM], F32, kind="ExternalOutput")
        dbg_o = nc.dram_tensor("dbg_o", [64, CH * HDIM], F32, kind="ExternalOutput")
        dbg_xa = nc.dram_tensor("dbg_xa", [128, NSEQ * HID], F32, kind="ExternalOutput")
        dbg_in = nc.dram_tensor("dbg_in", [128, INNER], F32, kind="ExternalOutput")
        dbg_ln = nc.dram_tensor("dbg_ln", [128, HID], BF16, kind="ExternalOutput")
        dbg_xt = nc.dram_tensor("dbg_xt", [128, 8 * 128], BF16, kind="ExternalOutput")
        dbg_pq = nc.dram_tensor("dbg_pq", [128, HID], F32, kind="ExternalOutput")
        dbg_ti = nc.dram_tensor("dbg_ti", [128, 72], F32, kind="ExternalOutput")
        dbg_xc = nc.dram_tensor("dbg_xc", [128, NSEQ * HID], F32, kind="ExternalOutput")

    prog = P()

    with ExitStack() as stack:
        brep = stack.enter_context(nc.sbuf_tensor("brep", [128, 4, HID], F32))
        xch = stack.enter_context(nc.sbuf_tensor("xch", [128, NSEQ, HID], F32))
        sqb = stack.enter_context(nc.sbuf_tensor("sq", [128, HID], F32))
        lnb = stack.enter_context(nc.sbuf_tensor("lnb", [128, HID], F32))
        xnb = stack.enter_context(nc.sbuf_tensor("xnb", [128, HID], BF16))
        xt = stack.enter_context(nc.sbuf_tensor("xt", [128, 8, 128], BF16))
        ptok = stack.enter_context(nc.sbuf_tensor("ptok", [128, HID], F32))
        wstr = stack.enter_context(nc.sbuf_tensor("wstr", [128, 2, HID], BF16))
        wfb = stack.enter_context(nc.sbuf_tensor("wfb", [128, 2, 8, 512], BF16))
        innb = stack.enter_context(nc.sbuf_tensor("inner", [128, INNER], F32))
        xq2 = stack.enter_context(nc.sbuf_tensor("xq2", [128, INNER], BF16))
        xq2t = stack.enter_context(nc.sbuf_tensor("xq2t", [128, 32, 128], BF16))
        ffout = stack.enter_context(nc.sbuf_tensor("ffout", [128, HID], F32))
        outb = stack.enter_context(nc.sbuf_tensor("outb", [128, HID], F32))
        ident = stack.enter_context(nc.sbuf_tensor("ident", [128, 128], BF16))
        tiny = stack.enter_context(nc.sbuf_tensor("tiny", [128, 72], F32))
        ones = stack.enter_context(nc.sbuf_tensor("ones", [128, 1], F32))
        mscb = stack.enter_context(nc.sbuf_tensor("msc", [128, 8], F32))
        kP = stack.enter_context(nc.sbuf_tensor("kP", [64, HB, HDIM], F32))
        qP = stack.enter_context(nc.sbuf_tensor("qP", [64, HB, HDIM], F32))
        vP = stack.enter_context(nc.sbuf_tensor("vP", [64, HB, HDIM], F32))
        S = stack.enter_context(nc.sbuf_tensor("S", [64, HDIM, HDIM], F32))
        scr = stack.enter_context(nc.sbuf_tensor("scr", [64, HDIM, HDIM], F32))
        opair = stack.enter_context(nc.sbuf_tensor("opair", [64, HB, HDIM], F32))
        vold = stack.enter_context(nc.sbuf_tensor("vold", [64, HDIM], F32))
        dcl = stack.enter_context(nc.sbuf_tensor("dcl", [64, HDIM], F32))
        mrow = stack.enter_context(nc.sbuf_tensor("mrow", [64, 2], F32))
        psF = stack.enter_context(nc.psum_tensor("psF", [128, 1024], F32))
        psP = stack.enter_context(nc.psum_tensor("psP", [128, 1024], F32))
        psT = stack.enter_context(nc.psum_tensor("psT", [128, 128], BF16))
        psM = stack.enter_context(nc.psum_tensor("psM", [128, 2], F32))
        dq = stack.enter_context(nc.semaphore("dq"))
        dg = stack.enter_context(nc.semaphore("dg"))
        sv = stack.enter_context(nc.semaphore("sv"))
        ss_ = stack.enter_context(nc.semaphore("ss"))
        st_ = stack.enter_context(nc.semaphore("st"))
        block = stack.enter_context(nc.Block())
        SEM = {"dq": dq, "dg": dg, "sv": sv, "ss": ss_, "st": st_}

        ENG_SEM = {"vector": "sv", "scalar": "ss", "tensor": "st",
                   "gpsimd": "sv", "sync": None}

        def op(eng, sem=None, waits=(), n=1):
            def deco(fn):
                if LOCKSTEP and sem is None and ENG_SEM[eng]:
                    sem2 = ENG_SEM[eng]
                else:
                    sem2 = sem
                if LOCKSTEP:
                    waits2 = [(k, prog.cur(k)) for k in ("sv", "ss", "st", "dq", "dg")]
                else:
                    waits2 = list(waits)
                tgt = prog.inc(sem2, n) if sem2 else None
                wl = tuple((s, v) for (s, v) in waits2 if v and v > 0)
                def run(eng_obj, fn=fn, wl=wl, sem=sem2, n=n):
                    for s_name, val in wl:
                        eng_obj.wait_ge(SEM[s_name], val)
                    r = fn(eng_obj)
                    if sem is not None:
                        if n == 16 or not USE_DRAIN:
                            r.then_inc(SEM[sem], n)
                        else:
                            eng_obj.drain().then_inc(SEM[sem], n)
                prog.emit(eng, run)
                return tgt
            return deco

        def dma(eng, sem, dst, src, waits=()):
            return op(eng, sem=sem, waits=waits, n=16)(
                lambda e, d=dst, s=src: e.dma_start(d, s))

        V = "vector"; SC = "scalar"; TE = "tensor"; SY = "sync"; GP = "gpsimd"

        tslot = {}

        def ts_slot(name):
            if name not in tslot:
                i = len(tslot)
                assert i < 56
                tslot[name] = tiny[:, i:i + 1]
            return tslot[name]

        l2ss = tiny[:, 56:56 + HEADS]

        op(GP, sem=None)(lambda g: g.memset(ident[:, :], 0.0))
        op(GP, sem=None)(lambda g: g.affine_select(
            out=ident[:, :], in_=ident[:, :], compare_op=ALU.not_equal,
            fill=1.0, base=0, pattern=[[-1, 128]], channel_multiplier=1))
        op(GP, sem=None)(lambda g: g.memset(ones[:, :], 1.0))
        op(GP, sem="sv")(lambda g: g.memset(S[:, :, :], 0.0))
        for i, b_d in enumerate(b_ds):
            dma(SY, "dq", brep[:, i, :], bass.AP(b_d, 0, [[0, 128], [1, HID]]))

        innf = innb[:, :]
        xcf = xch[:, :, :].rearrange("p a b -> p (a b)")

        def quant_weight(w_d, wq_dd, nchunk, ncol, idx):
            acc = ts_slot("wacc")
            red = ts_slot("wred")
            op(V, sem=None)(lambda v: v.memset(acc, 0.0))
            for j in range(nchunk):
                ld = dma(SY, "dq", innf[:, :ncol],
                         w_d.rearrange("(c p) o -> p c o", p=128)[:, j, :],
                         waits=[("sv", prog.cur("sv"))])
                op(V, sem=None, waits=[("dq", ld)])(
                    lambda v: v.tensor_reduce(red, innf[:, :ncol], AX.X, ALU.add,
                                              apply_absolute_value=True))
                op(V, sem="sv")(lambda v: v.tensor_tensor(acc, acc, red, ALU.add))
            t_mm = op(TE, sem="st", waits=[("sv", prog.cur("sv"))])(
                lambda t: t.matmul(psM[0:1, 0:1], acc, ones[:, :], start=True, stop=True))
            mm_t = ts_slot("wmm")
            rec_t = ts_slot("wrec")
            op(V, sem=None, waits=[("st", t_mm)])(
                lambda v: v.tensor_scalar(mm_t[0:1, :], psM[0:1, 0:1],
                                          1.0 / (nchunk * 128 * ncol), EPS_WQ,
                                          ALU.mult, ALU.max))
            op(V, sem="sv")(lambda v: v.reciprocal(rec_t[0:1, :], mm_t[0:1, :]))
            dma(GP, "dg", msc_d[2 * idx:2 * idx + 1], mm_t[0:1, :],
                waits=[("sv", prog.cur("sv"))])
            d2 = dma(GP, "dg", msc_d[2 * idx + 1:2 * idx + 2], rec_t[0:1, :])
            dma(GP, "dg", mscb[:, 2 * idx:2 * idx + 1],
                bass.AP(msc_d, 2 * idx, [[0, 128], [1, 1]]), waits=[("dg", d2)])
            d4 = dma(GP, "dg", mscb[:, 2 * idx + 1:2 * idx + 2],
                     bass.AP(msc_d, 2 * idx + 1, [[0, 128], [1, 1]]))
            for j in range(nchunk):
                ld = dma(SY, "dq", innf[:, :ncol],
                         w_d.rearrange("(c p) o -> p c o", p=128)[:, j, :],
                         waits=[("sv", prog.cur("sv")), ("dg", d4)])
                op(V, sem=None, waits=[("dq", ld), ("dg", prog.cur("dg"))])(
                    lambda v: v.tensor_scalar(xcf[:, :ncol], innf[:, :ncol],
                                              mscb[:, 2 * idx + 1:2 * idx + 2], MAGIC,
                                              ALU.mult, ALU.add))
                op(V, sem=None)(lambda v: v.tensor_scalar(
                    xcf[:, :ncol], xcf[:, :ncol], MAGIC, -1.0, ALU.subtract, ALU.max))
                vq = op(V, sem="sv")(lambda v: v.tensor_scalar(
                    xq2[:, :ncol], xcf[:, :ncol], 1.0, None, ALU.min))
                dma(GP, "dg", wq_dd.rearrange("(c p) o -> p c o", p=128)[:, j, :],
                    xq2[:, :ncol], waits=[("sv", vq)])

        quant_weight(wg_d, wgq_d, 8, INNER, 0)
        quant_weight(wu_d, wuq_d, 8, INNER, 1)
        quant_weight(wd_d, wdq_d, 32, HID, 2)

        def emit_ln(src_ap, out_ap, pref):
            sums = ts_slot(pref + "_s"); ssum = ts_slot(pref + "_q")
            m1 = ts_slot(pref + "_m"); nv = ts_slot(pref + "_n")
            ve = ts_slot(pref + "_v"); sd = ts_slot(pref + "_d")
            rs = ts_slot(pref + "_r")
            op(V, sem="sv")(lambda v: v.tensor_reduce(sums, src_ap, AX.X, ALU.add))
            c_sq = op(SC, sem="ss", waits=[("sv", prog.cur("sv"))])(
                lambda s: s.activation(sqb[:, :], src_ap, ACT.Square))
            op(V, sem=None, waits=[("ss", c_sq)])(
                lambda v: v.tensor_reduce(ssum, sqb[:, :], AX.X, ALU.add))
            op(V, sem=None)(lambda v: v.tensor_scalar(m1, sums, 1.0 / HID, None, ALU.mult))
            op(V, sem=None)(lambda v: v.tensor_scalar(ssum, ssum, 1.0 / HID, None, ALU.mult))
            op(V, sem=None)(lambda v: v.scalar_tensor_tensor(
                nv, m1, m1, ssum, ALU.mult, ALU.subtract))
            op(V, sem="sv")(lambda v: v.tensor_scalar(
                ve, nv, -1.0, EPS_LN, ALU.mult, ALU.add))
            c_rt = op(SC, sem="ss", waits=[("sv", prog.cur("sv"))])(
                lambda s: s.activation(sd, ve, ACT.Sqrt))
            op(V, sem=None, waits=[("ss", c_rt)])(lambda v: v.reciprocal(rs, sd))
            return op(V, sem="sv", waits=[("st", prog.cur("st"))])(
                lambda v: v.tensor_scalar(
                    out_ap, src_ap, m1, rs, ALU.subtract, ALU.mult))

        def emit_transposes(src_ap, dst, ntile):
            last = None
            for j in range(ntile):
                tt = op(TE, sem="st", waits=[("sv", prog.cur("sv"))])(
                    lambda t, j=j: t.transpose(
                        psT[:, :], src_ap[:, j * 128:(j + 1) * 128], ident[:, :]))
                last = op(V, sem="sv", waits=[("st", tt)])(
                    lambda v, j=j: v.tensor_copy(dst[:, j, :], psT[:, :]))
            return last

        def emit_proj(w_dram, bias_idx, out_ap):
            last_mm = None
            for j in range(8):
                ld = dma(SY, "dq", innf[:, :HID],
                         w_dram.rearrange("(c p) o -> p c o", p=128)[:, j, :],
                         waits=[("sv", prog.cur("sv"))])
                cv = op(V, sem="sv", waits=[("dq", ld), ("st", prog.cur("st"))])(
                    lambda v, j=j: v.tensor_copy(wstr[:, j % 2, :], innf[:, :HID]))
                for nh in range(2):
                    last_mm = op(TE, sem="st", waits=[("sv", cv)] if nh == 0 else ())(
                        lambda t, j=j, nh=nh: t.matmul(
                            psP[:, nh * 512:(nh + 1) * 512], xt[:, j, :],
                            wstr[:, j % 2, nh * 512:(nh + 1) * 512],
                            start=(j == 0), stop=(j == 7)))
            ev = None
            for nh in range(2):
                ev = op(V, sem="sv",
                        waits=[("st", last_mm), ("dg", prog.cur("dg"))] if nh == 0 else ())(
                    lambda v, nh=nh: v.tensor_tensor(
                        out_ap[:, nh * 512:(nh + 1) * 512],
                        psP[:, nh * 512:(nh + 1) * 512],
                        brep[:, bias_idx, nh * 512:(nh + 1) * 512], ALU.add))
            return ev

        def emit_l2(p_ap):
            c_sq = op(SC, sem="ss", waits=[("sv", prog.cur("sv"))])(
                lambda s: s.activation(sqb[:, :], p_ap, ACT.Square))
            op(V, sem="sv", waits=[("ss", c_sq)])(
                lambda v: v.tensor_reduce(
                    l2ss, sqb[:, :].rearrange("p (h d) -> p h d", h=HEADS),
                    AX.X, ALU.add))
            c_rt = op(SC, sem="ss", waits=[("sv", prog.cur("sv"))])(
                lambda s: s.activation(l2ss, l2ss, ACT.Sqrt))
            op(V, sem=None, waits=[("ss", c_rt)])(
                lambda v: v.tensor_scalar(l2ss, l2ss, EPS_L2, None, ALU.max))
            op(V, sem=None)(lambda v: v.reciprocal(l2ss, l2ss))
            return op(V, sem="sv")(lambda v: v.tensor_tensor(
                p_ap.rearrange("p (h d) -> p h d", h=HEADS),
                p_ap.rearrange("p (h d) -> p h d", h=HEADS),
                l2ss.unsqueeze(2).broadcast_to([128, HEADS, HDIM]), ALU.mult))

        def emit_actquant(src_ap, ncols, out_bf, sxr_slot, pref):
            amax = ts_slot(pref + "_a"); sx = ts_slot(pref + "_x")
            op(V, sem=None)(lambda v: v.tensor_reduce(
                amax, src_ap, AX.X, ALU.max, apply_absolute_value=True))
            op(V, sem=None)(lambda v: v.tensor_scalar(amax, amax, EPS_AQ, None, ALU.max))
            op(V, sem=None)(lambda v: v.reciprocal(sx, amax))
            op(V, sem=None)(lambda v: v.tensor_scalar(sx, sx, 127.0, None, ALU.mult))
            op(V, sem=None)(lambda v: v.tensor_scalar(
                sxr_slot, amax, 1.0 / 127.0, None, ALU.mult))
            last = None
            first = True
            for qtr in range(0, ncols, HID):
                sl = slice(qtr, qtr + HID)
                op(V, sem=None,
                   waits=[("st", prog.cur("st"))] if first else ())(
                    lambda v, sl=sl: v.tensor_scalar(
                        sqb[:, :], src_ap[:, sl], sx, MAGIC, ALU.mult, ALU.add))
                first = False
                op(V, sem=None)(lambda v: v.tensor_scalar(
                    sqb[:, :], sqb[:, :], MAGIC, -128.0, ALU.subtract, ALU.max))
                last = op(V, sem="sv")(lambda v, sl=sl: v.tensor_scalar(
                    out_bf[:, sl], sqb[:, :], 127.0, None, ALU.min))
            return last

        for c in range(NCH):
            for s in range(NSEQ):
                dma(SY, "dq", xch[:, s, :],
                    x_in[s * T + c * CH: s * T + c * CH + CH, :],
                    waits=[("sv", prog.cur("sv"))] if s == 0 else ())

            xw = prog.cur("dq")
            for s in range(NSEQ):
                if s == 0:
                    op(V, sem="sv", waits=[("dq", xw)])(lambda v: v.nop())
                emit_ln(xch[:, s, :], xnb[:, :], "ln1")
                if dbg and c == 0 and s == 0:
                    dma(GP, "dg", dbg_ti[:, :], tiny[:, :],
                        waits=[("sv", prog.cur("sv"))])
                    dd = dma(GP, "dg", dbg_xc[:, :],
                             xch[:, :, :].rearrange("p a b -> p (a b)"))
                    op(V, sem="sv", waits=[("dg", dd)])(lambda v: v.nop())
                emit_transposes(xnb[:, :], xt, 8)
                if dbg and c == 0 and s == 0:
                    dma(GP, "dg", dbg_ln[:, :], xnb[:, :],
                        waits=[("sv", prog.cur("sv"))])
                    dma(GP, "dg", dbg_xt[:, :],
                        xt[:, :, :].rearrange("p a b -> p (a b)"))
                for (w_dram, bi, scdst, do_l2) in (
                        (wq_d, 0, qsc_d, True), (wk_d, 1, ksc_d, True),
                        (wv_d, 2, vsc_d, False)):
                    ev = emit_proj(w_dram, bi, ptok)
                    if dbg and c == 0 and s == 0 and bi == 0:
                        dma(GP, "dg", dbg_pq[:, :], ptok[:, :],
                            waits=[("sv", prog.cur("sv"))])
                    if do_l2:
                        ev = emit_l2(ptok[:, :])
                    dma(GP, "dg", scdst[:, s * HID:(s + 1) * HID], ptok[:, :],
                        waits=[("sv", ev)])

            for hb in range(CH // HB):
                def rd_ap(scr_d, hb=hb):
                    return bass.AP(scr_d, hb * HB * (NSEQ * HID),
                                   [[64, 64], [NSEQ * HID, HB], [1, 64]])
                dma(GP, "dg", kP[:, :, :], rd_ap(ksc_d),
                    waits=[("sv", prog.cur("sv"))])
                dma(GP, "dg", qP[:, :, :], rd_ap(qsc_d))
                r3 = dma(GP, "dg", vP[:, :, :], rd_ap(vsc_d))
                if dbg and c == 0 and hb == 0:
                    dma(GP, "dg", dbg_q[:, :], qsc_d[:, :])
                    r3 = dma(GP, "dg", dbg_kp[:, :],
                             kP[:, :, :].rearrange("p a b -> p (a b)"),
                             waits=[("dg", r3)])

                for t_ in range(HB):
                    w0 = [("dg", r3)] if t_ == 0 else ()
                    op(V, sem=None, waits=w0)(lambda v, t=t_: v.tensor_tensor(
                        scr[:, :, :], S[:, :, :].transpose([0, 2, 1]),
                        kP[:, t, :].unsqueeze(1).broadcast_to([64, HDIM, HDIM]),
                        ALU.mult))
                    op(V, sem=None)(lambda v: v.tensor_reduce(
                        vold[:, :], scr[:, :, :], AX.X, ALU.add))
                    op(V, sem=None)(lambda v, t=t_: v.scalar_tensor_tensor(
                        dcl[:, :], vold[:, :], 1.0, vP[:, t, :], ALU.mult, ALU.subtract))
                    op(V, sem=None)(lambda v: v.tensor_scalar(
                        dcl[:, :], dcl[:, :], -1.0, 1.0, ALU.max, ALU.min))
                    op(V, sem=None)(lambda v, t=t_: v.scalar_tensor_tensor(
                        scr[:, :, :],
                        kP[:, t, :].unsqueeze(2).broadcast_to([64, HDIM, HDIM]), -BETA,
                        dcl[:, :].unsqueeze(1).broadcast_to([64, HDIM, HDIM]),
                        ALU.mult, ALU.mult))
                    op(V, sem=None)(lambda v: v.tensor_tensor(
                        S[:, :, :], S[:, :, :], scr[:, :, :], ALU.add))
                    op(V, sem=None)(lambda v: v.tensor_reduce(
                        mrow[:, 0:1], S[:, :, :], AX.XY, ALU.max,
                        apply_absolute_value=True))
                    op(V, sem=None)(lambda v: v.tensor_scalar(
                        mrow[:, 0:1], mrow[:, 0:1], 10.0, None, ALU.is_gt))
                    op(V, sem=None)(lambda v: v.tensor_scalar(
                        mrow[:, 0:1], mrow[:, 0:1], -0.1, 1.0, ALU.mult, ALU.add))
                    op(V, sem=None)(lambda v: v.tensor_scalar(
                        S[:, :, :].rearrange("p a b -> p (a b)"),
                        S[:, :, :].rearrange("p a b -> p (a b)"),
                        mrow[:, 0:1], None, ALU.mult))
                    op(V, sem=None)(lambda v, t=t_: v.tensor_tensor(
                        scr[:, :, :], S[:, :, :],
                        qP[:, t, :].unsqueeze(1).broadcast_to([64, HDIM, HDIM]),
                        ALU.mult))
                    op(V, sem=None)(lambda v: v.tensor_reduce(
                        vold[:, :], scr[:, :, :], AX.X, ALU.add))
                    op(V, sem=("sv" if t_ == HB - 1 else None))(
                        lambda v, t=t_: v.tensor_scalar(
                            opair[:, t, :], vold[:, :], -1.0, 1.0, ALU.max, ALU.min))
                dma(GP, "dg",
                    bass.AP(osc_d, hb * HB * HDIM,
                            [[CH * HDIM, 64], [HDIM, HB], [1, 64]]),
                    opair[:, :, :], waits=[("sv", prog.cur("sv"))])

            og = prog.cur("dg")
            for s in range(NSEQ):
                r_o = dma(GP, "dg", ptok[:, :],
                          bass.AP(osc_d, s * HEADS * CH * HDIM,
                                  [[HDIM, CH], [CH * HDIM, HEADS], [1, HDIM]]),
                          waits=[("dg", og), ("sv", prog.cur("sv"))] if s == 0
                          else [("sv", prog.cur("sv"))])
                op(V, sem="sv", waits=[("dg", r_o), ("st", prog.cur("st"))])(
                    lambda v: v.tensor_copy(xnb[:, :], ptok[:, :]))
                emit_transposes(xnb[:, :], xt, 8)
                emit_proj(wo_d, 3, ptok)
                emit_ln(ptok[:, :], lnb[:, :], "ln2")
                op(V, sem="sv")(lambda v, s=s: v.tensor_tensor(
                    xch[:, s, :], xch[:, s, :], lnb[:, :], ALU.add))

            if dbg and c == 0:
                dma(GP, "dg", dbg_o[:, :], osc_d[:, :],
                    waits=[("sv", prog.cur("sv"))])
                dma(GP, "dg", dbg_xa[:, :],
                    xch[:, :, :].rearrange("p a b -> p (a b)"),
                    waits=[("sv", prog.cur("sv"))])
            for s in range(NSEQ):
                emit_ln(xch[:, s, :], lnb[:, :], "ln3")
                sxr = ts_slot("aq1_sr")
                emit_actquant(lnb[:, :], HID, xnb, sxr, "aq1")
                emit_transposes(xnb[:, :], xt, 8)
                cg = ts_slot("aq1_cg"); cu = ts_slot("aq1_cu")
                op(V, sem=None)(lambda v: v.tensor_tensor(cg, sxr, mscb[:, 0:1], ALU.mult))
                op(V, sem="sv")(lambda v: v.tensor_tensor(cu, sxr, mscb[:, 2:3], ALU.mult))

                for which, wsrc, comb in (("g", wgq_d, cg), ("u", wuq_d, cu)):
                    for nb in range(8):
                        src = bass.AP(wsrc, nb * 512,
                                      [[INNER, 128], [128 * INNER, 8], [1, 512]])
                        ld = dma(SY, "dq", wfb[:, nb % 2, :, :], src,
                                 waits=[("st", prog.cur("st"))])
                        lm = None
                        for j in range(8):
                            lm = op(TE, sem="st",
                                    waits=[("dq", ld), ("sv", prog.cur("sv")),
                                           ("ss", prog.cur("ss"))]
                                    if j == 0 else ())(
                                lambda t, j=j, nb=nb: t.matmul(
                                    psF[:, (nb % 2) * 512:(nb % 2) * 512 + 512],
                                    xt[:, j, :], wfb[:, nb % 2, j, :],
                                    start=(j == 0), stop=(j == 7)))
                        if which == "g":
                            op(SC, sem="ss", waits=[("st", lm)])(
                                lambda s_, nb=nb, comb=comb: s_.activation(
                                    innb[:, nb * 512:(nb + 1) * 512],
                                    psF[:, (nb % 2) * 512:(nb % 2) * 512 + 512],
                                    ACT.Silu, scale=comb))
                        else:
                            op(V, sem="sv", waits=[("st", lm), ("ss", prog.cur("ss"))])(
                                lambda v, nb=nb, comb=comb: v.scalar_tensor_tensor(
                                    innb[:, nb * 512:(nb + 1) * 512],
                                    psF[:, (nb % 2) * 512:(nb % 2) * 512 + 512],
                                    comb, innb[:, nb * 512:(nb + 1) * 512],
                                    ALU.mult, ALU.mult))
                    if which == "g":
                        op(TE, sem=None, waits=[("ss", prog.cur("ss"))])(lambda t: t.nop())
                    else:
                        op(TE, sem=None, waits=[("sv", prog.cur("sv"))])(lambda t: t.nop())

                sxr2 = ts_slot("aq2_sr")
                if dbg and c == 0 and s == 0:
                    dma(GP, "dg", dbg_in[:, :], innb[:, :],
                        waits=[("sv", prog.cur("sv")), ("ss", prog.cur("ss"))])
                op(V, sem=None, waits=[("ss", prog.cur("ss"))])(lambda v: v.nop())
                emit_actquant(innb[:, :], INNER, xq2, sxr2, "aq2")
                emit_transposes(xq2[:, :], xq2t, 32)
                cd = ts_slot("aq2_cd")
                op(V, sem="sv")(lambda v: v.tensor_tensor(cd, sxr2, mscb[:, 4:5], ALU.mult))
                lm = None
                for nb2 in range(2):
                    for jg in range(4):
                        src = bass.AP(wdq_d, nb2 * 512 + jg * 8 * 128 * HID,
                                      [[HID, 128], [128 * HID, 8], [1, 512]])
                        ld = dma(SY, "dq", wfb[:, jg % 2, :, :], src,
                                 waits=[("st", prog.cur("st"))])
                        for j in range(8):
                            lm = op(TE, sem="st",
                                    waits=[("dq", ld), ("sv", prog.cur("sv"))]
                                    if j == 0 else ())(
                                lambda t, j=j, jg=jg, nb2=nb2: t.matmul(
                                    psF[:, nb2 * 512:(nb2 + 1) * 512],
                                    xq2t[:, jg * 8 + j, :], wfb[:, jg % 2, j, :],
                                    start=(jg == 0 and j == 0),
                                    stop=(jg == 3 and j == 7)))
                    op(V, sem="sv", waits=[("st", lm)])(
                        lambda v, nb2=nb2: v.tensor_scalar(
                            ffout[:, nb2 * 512:(nb2 + 1) * 512],
                            psF[:, nb2 * 512:(nb2 + 1) * 512], cd, None, ALU.mult))
                    op(TE, sem=None, waits=[("sv", prog.cur("sv"))])(lambda t: t.nop())
                emit_ln(ffout[:, :], lnb[:, :], "ln4")
                ov = op(V, sem="sv", waits=[("dq", prog.cur("dq"))])(
                    lambda v, s=s: v.tensor_tensor(
                        outb[:, :], xch[:, s, :], lnb[:, :], ALU.add))
                dma(SY, "dq", y_out[s * T + c * CH: s * T + c * CH + CH, :],
                    outb[:, :], waits=[("sv", ov)])

        op(SY, sem=None, waits=[("dq", prog.cur("dq")), ("dg", prog.cur("dg"))])(
            lambda e: e.nop())

        fin = {k: prog.cur(k) for k in ("dq", "dg", "sv", "ss", "st")}

        @block.gpsimd
        def _(eng):
            if USE_BOOT:
                for h in (dq, dg, sv, ss_, st_):
                    eng.sem_clear(h)
            eng.drain().then_inc(sv, 1)
            for fn in prog.ops["gpsimd"]:
                fn(eng)
            if USE_EXIT:
                for k in ("dq", "dg", "sv", "ss", "st"):
                    if fin[k]:
                        eng.wait_ge(SEM[k], fin[k])
                for h in (dq, dg, sv, ss_, st_):
                    eng.sem_clear(h)

        @block.sync
        def _(eng):
            for _i in range(256):
                eng.nop()
            eng.wait_ge(sv, 1)
            for fn in prog.ops["sync"]:
                fn(eng)

        @block.vector
        def _(eng):
            for _i in range(256):
                eng.nop()
            eng.wait_ge(sv, 1)
            for fn in prog.ops["vector"]:
                fn(eng)

        @block.scalar
        def _(eng):
            for _i in range(256):
                eng.nop()
            eng.wait_ge(sv, 1)
            for fn in prog.ops["scalar"]:
                fn(eng)

        @block.tensor
        def _(eng):
            for _i in range(256):
                eng.nop()
            eng.wait_ge(sv, 1)
            for fn in prog.ops["tensor"]:
                fn(eng)

    nc._prog_cnt = dict(prog.cnt)
    return nc


def make_in_maps(inputs, T):
    x = np.ascontiguousarray(np.asarray(inputs["x"], dtype=np.float32))
    per = x.shape[0] // NCORES
    common = {k: v for k, v in inputs.items() if k != "x"}
    in_maps = []
    for c in range(NCORES):
        m = dict(common)
        m["x"] = x[c * per:(c + 1) * per].reshape(per * T, HID)
        in_maps.append(m)
    return in_maps, per


def kernel(L0_attn_hidden_states, L0_attn_Wq, L0_attn_bq, L0_attn_Wk, L0_attn_bk,
           L0_attn_Wv, L0_attn_bv, L0_attn_Wo, L0_attn_bo,
           L0_ffn_w_gate, L0_ffn_w_up, L0_ffn_w_down):
    x = np.asarray(L0_attn_hidden_states, dtype=np.float32)
    B, T, H = x.shape
    nc = build_program(T=T)
    inputs = {
        "x": x,
        "wq": np.ascontiguousarray(np.asarray(L0_attn_Wq, np.float32).T),
        "wk": np.ascontiguousarray(np.asarray(L0_attn_Wk, np.float32).T),
        "wv": np.ascontiguousarray(np.asarray(L0_attn_Wv, np.float32).T),
        "wo": np.ascontiguousarray(np.asarray(L0_attn_Wo, np.float32).T),
        "bq": np.asarray(L0_attn_bq, np.float32), "bk": np.asarray(L0_attn_bk, np.float32),
        "bv": np.asarray(L0_attn_bv, np.float32), "bo": np.asarray(L0_attn_bo, np.float32),
        "wg": np.ascontiguousarray(np.asarray(L0_ffn_w_gate, np.float32).T),
        "wu": np.ascontiguousarray(np.asarray(L0_ffn_w_up, np.float32).T),
        "wd": np.ascontiguousarray(np.asarray(L0_ffn_w_down, np.float32).T),
    }
    in_maps, per = make_in_maps(inputs, T)
    res = run_bass_kernel_spmd(nc, in_maps, core_ids=list(range(NCORES)))
    out = np.concatenate([r["y"].reshape(per, T, H) for r in res.results], axis=0)
    return np.ascontiguousarray(out.astype(np.float32))


# version 03
# speedup vs baseline: 1.2571x; 1.2571x over previous
import numpy as np
from contextlib import ExitStack
import concourse.bass as bass
import concourse.mybir as mybir
from concourse.bass_utils import run_bass_kernel_spmd

F32 = mybir.dt.float32
BF16 = mybir.dt.bfloat16
ALU = mybir.AluOpType
ACT = mybir.ActivationFunctionType
AX = mybir.AxisListType

USE_DRAIN = True
USE_BOOT = True
USE_EXIT = False
LOCKSTEP = False
RELAX_REGIONS = set()
MAGIC = 12582912.0
NCORES = 8
NSEQ = 4
HID = 1024
HEADS = 16
HDIM = 64
INNER = 4096
CH = 128
HB = 64
BETA = 0.9
EPS_LN = 1e-5
EPS_L2 = 1e-12
EPS_WQ = 1e-5
EPS_AQ = 1e-5


class P:
    def __init__(self):
        self.ops = {"sync": [], "gpsimd": [], "vector": [], "scalar": [], "tensor": []}
        self.cnt = {"sv": 1}
        self.region = "init"

    def emit(self, eng, fn):
        self.ops[eng].append(fn)

    def inc(self, sem, n=1):
        self.cnt[sem] = self.cnt.get(sem, 0) + n
        return self.cnt[sem]

    def cur(self, sem):
        return self.cnt.get(sem, 0)


def build_program(T=2048, dbg=False):
    nc = bass.Bass("TRN2", target_bir_lowering=False, detect_race_conditions=False)
    NCH = T // CH

    x_in = nc.dram_tensor("x", [NSEQ * T, HID], F32, kind="ExternalInput")
    wq_d = nc.dram_tensor("wq", [HID, HID], F32, kind="ExternalInput")
    wk_d = nc.dram_tensor("wk", [HID, HID], F32, kind="ExternalInput")
    wv_d = nc.dram_tensor("wv", [HID, HID], F32, kind="ExternalInput")
    wo_d = nc.dram_tensor("wo", [HID, HID], F32, kind="ExternalInput")
    b_ds = [nc.dram_tensor(n, [HID], F32, kind="ExternalInput")
            for n in ("bq", "bk", "bv", "bo")]
    wg_d = nc.dram_tensor("wg", [HID, INNER], F32, kind="ExternalInput")
    wu_d = nc.dram_tensor("wu", [HID, INNER], F32, kind="ExternalInput")
    wd_d = nc.dram_tensor("wd", [INNER, HID], F32, kind="ExternalInput")
    y_out = nc.dram_tensor("y", [NSEQ * T, HID], F32, kind="ExternalOutput")

    wgq_d = nc.dram_tensor("wgq", [HID, INNER], BF16)
    wuq_d = nc.dram_tensor("wuq", [HID, INNER], BF16)
    wdq_d = nc.dram_tensor("wdq", [INNER, HID], BF16)
    qsc_d = nc.dram_tensor("qsc", [CH, NSEQ * HID], F32)
    ksc_d = nc.dram_tensor("ksc", [CH, NSEQ * HID], F32)
    vsc_d = nc.dram_tensor("vsc", [CH, NSEQ * HID], F32)
    osc_d = nc.dram_tensor("osc", [64, CH * HDIM], F32)
    msc_d = nc.dram_tensor("mscd", [8], F32)
    if dbg:
        dbg_q = nc.dram_tensor("dbg_q", [CH, NSEQ * HID], F32, kind="ExternalOutput")
        dbg_kp = nc.dram_tensor("dbg_kp", [64, HB * HDIM], F32, kind="ExternalOutput")
        dbg_o = nc.dram_tensor("dbg_o", [64, CH * HDIM], F32, kind="ExternalOutput")
        dbg_xa = nc.dram_tensor("dbg_xa", [128, NSEQ * HID], F32, kind="ExternalOutput")
        dbg_in = nc.dram_tensor("dbg_in", [128, INNER], F32, kind="ExternalOutput")
        dbg_ln = nc.dram_tensor("dbg_ln", [128, HID], BF16, kind="ExternalOutput")
        dbg_xt = nc.dram_tensor("dbg_xt", [128, 8 * 128], BF16, kind="ExternalOutput")
        dbg_pq = nc.dram_tensor("dbg_pq", [128, HID], F32, kind="ExternalOutput")
        dbg_ti = nc.dram_tensor("dbg_ti", [128, 72], F32, kind="ExternalOutput")
        dbg_xc = nc.dram_tensor("dbg_xc", [128, NSEQ * HID], F32, kind="ExternalOutput")

    prog = P()

    with ExitStack() as stack:
        brep = stack.enter_context(nc.sbuf_tensor("brep", [128, 4, HID], F32))
        xch = stack.enter_context(nc.sbuf_tensor("xch", [128, NSEQ, HID], F32))
        sqb = stack.enter_context(nc.sbuf_tensor("sq", [128, HID], F32))
        lnb = stack.enter_context(nc.sbuf_tensor("lnb", [128, HID], F32))
        xnb = stack.enter_context(nc.sbuf_tensor("xnb", [128, HID], BF16))
        xt = stack.enter_context(nc.sbuf_tensor("xt", [128, 8, 128], BF16))
        ptok = stack.enter_context(nc.sbuf_tensor("ptok", [128, HID], F32))
        wstr = stack.enter_context(nc.sbuf_tensor("wstr", [128, 2, HID], BF16))
        wfb = stack.enter_context(nc.sbuf_tensor("wfb", [128, 2, 8, 512], BF16))
        innb = stack.enter_context(nc.sbuf_tensor("inner", [128, INNER], F32))
        xq2 = stack.enter_context(nc.sbuf_tensor("xq2", [128, INNER], BF16))
        xq2t = stack.enter_context(nc.sbuf_tensor("xq2t", [128, 32, 128], BF16))
        ffout = stack.enter_context(nc.sbuf_tensor("ffout", [128, HID], F32))
        outb = stack.enter_context(nc.sbuf_tensor("outb", [128, HID], F32))
        ident = stack.enter_context(nc.sbuf_tensor("ident", [128, 128], BF16))
        tiny = stack.enter_context(nc.sbuf_tensor("tiny", [128, 72], F32))
        ones = stack.enter_context(nc.sbuf_tensor("ones", [128, 1], F32))
        mscb = stack.enter_context(nc.sbuf_tensor("msc", [128, 8], F32))
        kP = stack.enter_context(nc.sbuf_tensor("kP", [64, HB, HDIM], F32))
        qP = stack.enter_context(nc.sbuf_tensor("qP", [64, HB, HDIM], F32))
        vP = stack.enter_context(nc.sbuf_tensor("vP", [64, HB, HDIM], F32))
        S = stack.enter_context(nc.sbuf_tensor("S", [64, HDIM, HDIM], F32))
        scr = stack.enter_context(nc.sbuf_tensor("scr", [64, HDIM, HDIM], F32))
        opair = stack.enter_context(nc.sbuf_tensor("opair", [64, HB, HDIM], F32))
        vold = stack.enter_context(nc.sbuf_tensor("vold", [64, HDIM], F32))
        dcl = stack.enter_context(nc.sbuf_tensor("dcl", [64, HDIM], F32))
        mrow = stack.enter_context(nc.sbuf_tensor("mrow", [64, 2], F32))
        psF = stack.enter_context(nc.psum_tensor("psF", [128, 1024], F32))
        psP = stack.enter_context(nc.psum_tensor("psP", [128, 1024], F32))
        psT = stack.enter_context(nc.psum_tensor("psT", [128, 128], BF16))
        psM = stack.enter_context(nc.psum_tensor("psM", [128, 2], F32))
        dq = stack.enter_context(nc.semaphore("dq"))
        dg = stack.enter_context(nc.semaphore("dg"))
        sv = stack.enter_context(nc.semaphore("sv"))
        ss_ = stack.enter_context(nc.semaphore("ss"))
        st_ = stack.enter_context(nc.semaphore("st"))
        block = stack.enter_context(nc.Block())
        SEM = {"dq": dq, "dg": dg, "sv": sv, "ss": ss_, "st": st_}

        ENG_SEM = {"vector": "sv", "scalar": "ss", "tensor": "st",
                   "gpsimd": "sv", "sync": None}

        def op(eng, sem=None, waits=(), n=1):
            def deco(fn):
                locked = LOCKSTEP and prog.region not in RELAX_REGIONS
                if locked and sem is None and ENG_SEM[eng]:
                    sem2 = ENG_SEM[eng]
                else:
                    sem2 = sem
                if locked:
                    waits2 = [(k, prog.cur(k)) for k in ("sv", "ss", "st", "dq", "dg")]
                else:
                    waits2 = list(waits)
                tgt = prog.inc(sem2, n) if sem2 else None
                wl = tuple((s, v) for (s, v) in waits2 if v and v > 0)
                def run(eng_obj, fn=fn, wl=wl, sem=sem2, n=n, eng=eng):
                    for s_name, val in wl:
                        eng_obj.wait_ge(SEM[s_name], val)
                    r = fn(eng_obj)
                    if n == 16:
                        if sem is not None:
                            r.then_inc(SEM[sem], n)
                        return
                    if eng in ("vector", "scalar", "gpsimd"):
                        d = eng_obj.drain()
                        if sem is not None:
                            d.then_inc(SEM[sem], n)
                    elif sem is not None:
                        eng_obj.drain().then_inc(SEM[sem], n)
                prog.emit(eng, run)
                return tgt
            return deco

        def dma(eng, sem, dst, src, waits=()):
            return op(eng, sem=sem, waits=waits, n=16)(
                lambda e, d=dst, s=src: e.dma_start(d, s))

        V = "vector"; SC = "scalar"; TE = "tensor"; SY = "sync"; GP = "gpsimd"

        tslot = {}

        def ts_slot(name):
            if name not in tslot:
                i = len(tslot)
                assert i < 56
                tslot[name] = tiny[:, i:i + 1]
            return tslot[name]

        l2ss = tiny[:, 56:56 + HEADS]

        op(GP, sem=None)(lambda g: g.memset(ident[:, :], 0.0))
        op(GP, sem=None)(lambda g: g.affine_select(
            out=ident[:, :], in_=ident[:, :], compare_op=ALU.not_equal,
            fill=1.0, base=0, pattern=[[-1, 128]], channel_multiplier=1))
        op(GP, sem=None)(lambda g: g.memset(ones[:, :], 1.0))
        op(GP, sem="sv")(lambda g: g.memset(S[:, :, :], 0.0))
        for i, b_d in enumerate(b_ds):
            dma(SY, "dq", brep[:, i, :], bass.AP(b_d, 0, [[0, 128], [1, HID]]))

        innf = innb[:, :]
        xcf = xch[:, :, :].rearrange("p a b -> p (a b)")

        def quant_weight(w_d, wq_dd, nchunk, ncol, idx):
            acc = ts_slot("wacc")
            red = ts_slot("wred")
            op(V, sem=None)(lambda v: v.memset(acc, 0.0))
            for j in range(nchunk):
                ld = dma(SY, "dq", innf[:, :ncol],
                         w_d.rearrange("(c p) o -> p c o", p=128)[:, j, :],
                         waits=[("sv", prog.cur("sv"))])
                op(V, sem=None, waits=[("dq", ld)])(
                    lambda v: v.tensor_reduce(red, innf[:, :ncol], AX.X, ALU.add,
                                              apply_absolute_value=True))
                op(V, sem="sv")(lambda v: v.tensor_tensor(acc, acc, red, ALU.add))
            t_mm = op(TE, sem="st", waits=[("sv", prog.cur("sv"))])(
                lambda t: t.matmul(psM[0:1, 0:1], acc, ones[:, :], start=True, stop=True))
            mm_t = ts_slot("wmm")
            rec_t = ts_slot("wrec")
            op(V, sem=None, waits=[("st", t_mm)])(
                lambda v: v.tensor_scalar(mm_t[0:1, :], psM[0:1, 0:1],
                                          1.0 / (nchunk * 128 * ncol), EPS_WQ,
                                          ALU.mult, ALU.max))
            op(V, sem="sv")(lambda v: v.reciprocal(rec_t[0:1, :], mm_t[0:1, :]))
            dma(GP, "dg", msc_d[2 * idx:2 * idx + 1], mm_t[0:1, :],
                waits=[("sv", prog.cur("sv"))])
            d2 = dma(GP, "dg", msc_d[2 * idx + 1:2 * idx + 2], rec_t[0:1, :])
            dma(GP, "dg", mscb[:, 2 * idx:2 * idx + 1],
                bass.AP(msc_d, 2 * idx, [[0, 128], [1, 1]]), waits=[("dg", d2)])
            d4 = dma(GP, "dg", mscb[:, 2 * idx + 1:2 * idx + 2],
                     bass.AP(msc_d, 2 * idx + 1, [[0, 128], [1, 1]]))
            for j in range(nchunk):
                ld = dma(SY, "dq", innf[:, :ncol],
                         w_d.rearrange("(c p) o -> p c o", p=128)[:, j, :],
                         waits=[("sv", prog.cur("sv")), ("dg", d4)])
                op(V, sem=None, waits=[("dq", ld), ("dg", prog.cur("dg"))])(
                    lambda v: v.tensor_scalar(xcf[:, :ncol], innf[:, :ncol],
                                              mscb[:, 2 * idx + 1:2 * idx + 2], MAGIC,
                                              ALU.mult, ALU.add))
                op(V, sem=None)(lambda v: v.tensor_scalar(
                    xcf[:, :ncol], xcf[:, :ncol], MAGIC, -1.0, ALU.subtract, ALU.max))
                vq = op(V, sem="sv")(lambda v: v.tensor_scalar(
                    xq2[:, :ncol], xcf[:, :ncol], 1.0, None, ALU.min))
                dma(GP, "dg", wq_dd.rearrange("(c p) o -> p c o", p=128)[:, j, :],
                    xq2[:, :ncol], waits=[("sv", vq)])

        prog.region = "quant"
        quant_weight(wg_d, wgq_d, 8, INNER, 0)
        quant_weight(wu_d, wuq_d, 8, INNER, 1)
        quant_weight(wd_d, wdq_d, 32, HID, 2)

        prog.region = "init"
        def emit_ln(src_ap, out_ap, pref):
            sums = ts_slot(pref + "_s"); ssum = ts_slot(pref + "_q")
            m1 = ts_slot(pref + "_m"); nv = ts_slot(pref + "_n")
            ve = ts_slot(pref + "_v"); sd = ts_slot(pref + "_d")
            rs = ts_slot(pref + "_r")
            op(V, sem="sv")(lambda v: v.tensor_reduce(sums, src_ap, AX.X, ALU.add))
            c_sq = op(SC, sem="ss", waits=[("sv", prog.cur("sv"))])(
                lambda s: s.activation(sqb[:, :], src_ap, ACT.Square))
            op(V, sem=None, waits=[("ss", c_sq)])(
                lambda v: v.tensor_reduce(ssum, sqb[:, :], AX.X, ALU.add))
            op(V, sem=None)(lambda v: v.tensor_scalar(m1, sums, 1.0 / HID, None, ALU.mult))
            op(V, sem=None)(lambda v: v.tensor_scalar(ssum, ssum, 1.0 / HID, None, ALU.mult))
            op(V, sem=None)(lambda v: v.scalar_tensor_tensor(
                nv, m1, m1, ssum, ALU.mult, ALU.subtract))
            op(V, sem="sv")(lambda v: v.tensor_scalar(
                ve, nv, -1.0, EPS_LN, ALU.mult, ALU.add))
            c_rt = op(SC, sem="ss", waits=[("sv", prog.cur("sv"))])(
                lambda s: s.activation(sd, ve, ACT.Sqrt))
            op(V, sem=None, waits=[("ss", c_rt)])(lambda v: v.reciprocal(rs, sd))
            return op(V, sem="sv", waits=[("st", prog.cur("st"))])(
                lambda v: v.tensor_scalar(
                    out_ap, src_ap, m1, rs, ALU.subtract, ALU.mult))

        def emit_transposes(src_ap, dst, ntile):
            last = None
            for j in range(ntile):
                tt = op(TE, sem="st", waits=[("sv", prog.cur("sv"))])(
                    lambda t, j=j: t.transpose(
                        psT[:, :], src_ap[:, j * 128:(j + 1) * 128], ident[:, :]))
                last = op(V, sem="sv", waits=[("st", tt)])(
                    lambda v, j=j: v.tensor_copy(dst[:, j, :], psT[:, :]))
            return last

        def emit_proj(w_dram, bias_idx, out_ap):
            last_mm = None
            for j in range(8):
                ld = dma(SY, "dq", innf[:, :HID],
                         w_dram.rearrange("(c p) o -> p c o", p=128)[:, j, :],
                         waits=[("sv", prog.cur("sv"))])
                cv = op(V, sem="sv", waits=[("dq", ld), ("st", prog.cur("st"))])(
                    lambda v, j=j: v.tensor_copy(wstr[:, j % 2, :], innf[:, :HID]))
                for nh in range(2):
                    last_mm = op(TE, sem="st", waits=[("sv", cv)] if nh == 0 else ())(
                        lambda t, j=j, nh=nh: t.matmul(
                            psP[:, nh * 512:(nh + 1) * 512], xt[:, j, :],
                            wstr[:, j % 2, nh * 512:(nh + 1) * 512],
                            start=(j == 0), stop=(j == 7)))
            ev = None
            for nh in range(2):
                ev = op(V, sem="sv",
                        waits=[("st", last_mm), ("dg", prog.cur("dg"))] if nh == 0 else ())(
                    lambda v, nh=nh: v.tensor_tensor(
                        out_ap[:, nh * 512:(nh + 1) * 512],
                        psP[:, nh * 512:(nh + 1) * 512],
                        brep[:, bias_idx, nh * 512:(nh + 1) * 512], ALU.add))
            return ev

        def emit_l2(p_ap):
            c_sq = op(SC, sem="ss", waits=[("sv", prog.cur("sv"))])(
                lambda s: s.activation(sqb[:, :], p_ap, ACT.Square))
            op(V, sem="sv", waits=[("ss", c_sq)])(
                lambda v: v.tensor_reduce(
                    l2ss, sqb[:, :].rearrange("p (h d) -> p h d", h=HEADS),
                    AX.X, ALU.add))
            c_rt = op(SC, sem="ss", waits=[("sv", prog.cur("sv"))])(
                lambda s: s.activation(l2ss, l2ss, ACT.Sqrt))
            op(V, sem=None, waits=[("ss", c_rt)])(
                lambda v: v.tensor_scalar(l2ss, l2ss, EPS_L2, None, ALU.max))
            op(V, sem=None)(lambda v: v.reciprocal(l2ss, l2ss))
            return op(V, sem="sv")(lambda v: v.tensor_tensor(
                p_ap.rearrange("p (h d) -> p h d", h=HEADS),
                p_ap.rearrange("p (h d) -> p h d", h=HEADS),
                l2ss.unsqueeze(2).broadcast_to([128, HEADS, HDIM]), ALU.mult))

        def emit_actquant(src_ap, ncols, out_bf, sxr_slot, pref):
            amax = ts_slot(pref + "_a"); sx = ts_slot(pref + "_x")
            op(V, sem=None)(lambda v: v.tensor_reduce(
                amax, src_ap, AX.X, ALU.max, apply_absolute_value=True))
            op(V, sem=None)(lambda v: v.tensor_scalar(amax, amax, EPS_AQ, None, ALU.max))
            op(V, sem=None)(lambda v: v.reciprocal(sx, amax))
            op(V, sem=None)(lambda v: v.tensor_scalar(sx, sx, 127.0, None, ALU.mult))
            op(V, sem=None)(lambda v: v.tensor_scalar(
                sxr_slot, amax, 1.0 / 127.0, None, ALU.mult))
            last = None
            first = True
            for qtr in range(0, ncols, HID):
                sl = slice(qtr, qtr + HID)
                op(V, sem=None,
                   waits=[("st", prog.cur("st"))] if first else ())(
                    lambda v, sl=sl: v.tensor_scalar(
                        sqb[:, :], src_ap[:, sl], sx, MAGIC, ALU.mult, ALU.add))
                first = False
                op(V, sem=None)(lambda v: v.tensor_scalar(
                    sqb[:, :], sqb[:, :], MAGIC, -128.0, ALU.subtract, ALU.max))
                last = op(V, sem="sv")(lambda v, sl=sl: v.tensor_scalar(
                    out_bf[:, sl], sqb[:, :], 127.0, None, ALU.min))
            return last

        for c in range(NCH):
            prog.region = "xload"
            for s in range(NSEQ):
                dma(SY, "dq", xch[:, s, :],
                    x_in[s * T + c * CH: s * T + c * CH + CH, :],
                    waits=[("sv", prog.cur("sv"))] if s == 0 else ())

            xw = prog.cur("dq")
            prog.region = "proj"
            for s in range(NSEQ):
                if s == 0:
                    op(V, sem="sv", waits=[("dq", xw)])(lambda v: v.nop())
                emit_ln(xch[:, s, :], xnb[:, :], "ln1")
                if dbg and c == 0 and s == 0:
                    dma(GP, "dg", dbg_ti[:, :], tiny[:, :],
                        waits=[("sv", prog.cur("sv"))])
                    dd = dma(GP, "dg", dbg_xc[:, :],
                             xch[:, :, :].rearrange("p a b -> p (a b)"))
                    op(V, sem="sv", waits=[("dg", dd)])(lambda v: v.nop())
                emit_transposes(xnb[:, :], xt, 8)
                if dbg and c == 0 and s == 0:
                    dma(GP, "dg", dbg_ln[:, :], xnb[:, :],
                        waits=[("sv", prog.cur("sv"))])
                    dma(GP, "dg", dbg_xt[:, :],
                        xt[:, :, :].rearrange("p a b -> p (a b)"))
                for (w_dram, bi, scdst, do_l2) in (
                        (wq_d, 0, qsc_d, True), (wk_d, 1, ksc_d, True),
                        (wv_d, 2, vsc_d, False)):
                    ev = emit_proj(w_dram, bi, ptok)
                    if dbg and c == 0 and s == 0 and bi == 0:
                        dma(GP, "dg", dbg_pq[:, :], ptok[:, :],
                            waits=[("sv", prog.cur("sv"))])
                    if do_l2:
                        ev = emit_l2(ptok[:, :])
                    dma(GP, "dg", scdst[:, s * HID:(s + 1) * HID], ptok[:, :],
                        waits=[("sv", ev)])

            prog.region = "scan"
            for hb in range(CH // HB):
                def rd_ap(scr_d, hb=hb):
                    return bass.AP(scr_d, hb * HB * (NSEQ * HID),
                                   [[64, 64], [NSEQ * HID, HB], [1, 64]])
                dma(GP, "dg", kP[:, :, :], rd_ap(ksc_d),
                    waits=[("sv", prog.cur("sv"))])
                dma(GP, "dg", qP[:, :, :], rd_ap(qsc_d))
                r3 = dma(GP, "dg", vP[:, :, :], rd_ap(vsc_d))
                if dbg and c == 0 and hb == 0:
                    dma(GP, "dg", dbg_q[:, :], qsc_d[:, :])
                    r3 = dma(GP, "dg", dbg_kp[:, :],
                             kP[:, :, :].rearrange("p a b -> p (a b)"),
                             waits=[("dg", r3)])

                for t_ in range(HB):
                    w0 = [("dg", r3)] if t_ == 0 else ()
                    op(V, sem=None, waits=w0)(lambda v, t=t_: v.tensor_tensor(
                        scr[:, :, :], S[:, :, :].transpose([0, 2, 1]),
                        kP[:, t, :].unsqueeze(1).broadcast_to([64, HDIM, HDIM]),
                        ALU.mult))
                    op(V, sem=None)(lambda v: v.tensor_reduce(
                        vold[:, :], scr[:, :, :], AX.X, ALU.add))
                    op(V, sem=None)(lambda v, t=t_: v.scalar_tensor_tensor(
                        dcl[:, :], vold[:, :], 1.0, vP[:, t, :], ALU.mult, ALU.subtract))
                    op(V, sem=None)(lambda v: v.tensor_scalar(
                        dcl[:, :], dcl[:, :], -1.0, 1.0, ALU.max, ALU.min))
                    op(V, sem=None)(lambda v, t=t_: v.scalar_tensor_tensor(
                        scr[:, :, :],
                        kP[:, t, :].unsqueeze(2).broadcast_to([64, HDIM, HDIM]), -BETA,
                        dcl[:, :].unsqueeze(1).broadcast_to([64, HDIM, HDIM]),
                        ALU.mult, ALU.mult))
                    op(V, sem=None)(lambda v: v.tensor_tensor(
                        S[:, :, :], S[:, :, :], scr[:, :, :], ALU.add))
                    op(V, sem=None)(lambda v: v.tensor_reduce(
                        mrow[:, 0:1], S[:, :, :], AX.XY, ALU.max,
                        apply_absolute_value=True))
                    op(V, sem=None)(lambda v: v.tensor_scalar(
                        mrow[:, 0:1], mrow[:, 0:1], 10.0, None, ALU.is_gt))
                    op(V, sem=None)(lambda v: v.tensor_scalar(
                        mrow[:, 0:1], mrow[:, 0:1], -0.1, 1.0, ALU.mult, ALU.add))
                    op(V, sem=None)(lambda v: v.tensor_scalar(
                        S[:, :, :].rearrange("p a b -> p (a b)"),
                        S[:, :, :].rearrange("p a b -> p (a b)"),
                        mrow[:, 0:1], None, ALU.mult))
                    op(V, sem=None)(lambda v, t=t_: v.tensor_tensor(
                        scr[:, :, :], S[:, :, :],
                        qP[:, t, :].unsqueeze(1).broadcast_to([64, HDIM, HDIM]),
                        ALU.mult))
                    op(V, sem=None)(lambda v: v.tensor_reduce(
                        vold[:, :], scr[:, :, :], AX.X, ALU.add))
                    op(V, sem=("sv" if t_ == HB - 1 else None))(
                        lambda v, t=t_: v.tensor_scalar(
                            opair[:, t, :], vold[:, :], -1.0, 1.0, ALU.max, ALU.min))
                dma(GP, "dg",
                    bass.AP(osc_d, hb * HB * HDIM,
                            [[CH * HDIM, 64], [HDIM, HB], [1, 64]]),
                    opair[:, :, :], waits=[("sv", prog.cur("sv"))])

            prog.region = "oproj"
            og = prog.cur("dg")
            for s in range(NSEQ):
                r_o = dma(GP, "dg", ptok[:, :],
                          bass.AP(osc_d, s * HEADS * CH * HDIM,
                                  [[HDIM, CH], [CH * HDIM, HEADS], [1, HDIM]]),
                          waits=[("dg", og), ("sv", prog.cur("sv"))] if s == 0
                          else [("sv", prog.cur("sv"))])
                op(V, sem="sv", waits=[("dg", r_o), ("st", prog.cur("st"))])(
                    lambda v: v.tensor_copy(xnb[:, :], ptok[:, :]))
                emit_transposes(xnb[:, :], xt, 8)
                emit_proj(wo_d, 3, ptok)
                emit_ln(ptok[:, :], lnb[:, :], "ln2")
                op(V, sem="sv")(lambda v, s=s: v.tensor_tensor(
                    xch[:, s, :], xch[:, s, :], lnb[:, :], ALU.add))

            if dbg and c == 0:
                dma(GP, "dg", dbg_o[:, :], osc_d[:, :],
                    waits=[("sv", prog.cur("sv"))])
                dma(GP, "dg", dbg_xa[:, :],
                    xch[:, :, :].rearrange("p a b -> p (a b)"),
                    waits=[("sv", prog.cur("sv"))])
            prog.region = "ffn"
            for s in range(NSEQ):
                emit_ln(xch[:, s, :], lnb[:, :], "ln3")
                sxr = ts_slot("aq1_sr")
                emit_actquant(lnb[:, :], HID, xnb, sxr, "aq1")
                emit_transposes(xnb[:, :], xt, 8)
                cg = ts_slot("aq1_cg"); cu = ts_slot("aq1_cu")
                op(V, sem=None)(lambda v: v.tensor_tensor(cg, sxr, mscb[:, 0:1], ALU.mult))
                op(V, sem="sv")(lambda v: v.tensor_tensor(cu, sxr, mscb[:, 2:3], ALU.mult))

                for which, wsrc, comb in (("g", wgq_d, cg), ("u", wuq_d, cu)):
                    for nb in range(8):
                        src = bass.AP(wsrc, nb * 512,
                                      [[INNER, 128], [128 * INNER, 8], [1, 512]])
                        ld = dma(SY, "dq", wfb[:, nb % 2, :, :], src,
                                 waits=[("st", prog.cur("st"))])
                        lm = None
                        for j in range(8):
                            lm = op(TE, sem="st",
                                    waits=[("dq", ld), ("sv", prog.cur("sv")),
                                           ("ss", prog.cur("ss"))]
                                    if j == 0 else ())(
                                lambda t, j=j, nb=nb: t.matmul(
                                    psF[:, (nb % 2) * 512:(nb % 2) * 512 + 512],
                                    xt[:, j, :], wfb[:, nb % 2, j, :],
                                    start=(j == 0), stop=(j == 7)))
                        if which == "g":
                            op(SC, sem="ss", waits=[("st", lm)])(
                                lambda s_, nb=nb, comb=comb: s_.activation(
                                    innb[:, nb * 512:(nb + 1) * 512],
                                    psF[:, (nb % 2) * 512:(nb % 2) * 512 + 512],
                                    ACT.Silu, scale=comb))
                        else:
                            op(V, sem="sv", waits=[("st", lm), ("ss", prog.cur("ss"))])(
                                lambda v, nb=nb, comb=comb: v.scalar_tensor_tensor(
                                    innb[:, nb * 512:(nb + 1) * 512],
                                    psF[:, (nb % 2) * 512:(nb % 2) * 512 + 512],
                                    comb, innb[:, nb * 512:(nb + 1) * 512],
                                    ALU.mult, ALU.mult))
                    if which == "g":
                        op(TE, sem=None, waits=[("ss", prog.cur("ss"))])(lambda t: t.nop())
                    else:
                        op(TE, sem=None, waits=[("sv", prog.cur("sv"))])(lambda t: t.nop())

                sxr2 = ts_slot("aq2_sr")
                if dbg and c == 0 and s == 0:
                    dma(GP, "dg", dbg_in[:, :], innb[:, :],
                        waits=[("sv", prog.cur("sv")), ("ss", prog.cur("ss"))])
                op(V, sem=None, waits=[("ss", prog.cur("ss"))])(lambda v: v.nop())
                emit_actquant(innb[:, :], INNER, xq2, sxr2, "aq2")
                emit_transposes(xq2[:, :], xq2t, 32)
                cd = ts_slot("aq2_cd")
                op(V, sem="sv")(lambda v: v.tensor_tensor(cd, sxr2, mscb[:, 4:5], ALU.mult))
                lm = None
                for nb2 in range(2):
                    for jg in range(4):
                        src = bass.AP(wdq_d, nb2 * 512 + jg * 8 * 128 * HID,
                                      [[HID, 128], [128 * HID, 8], [1, 512]])
                        ld = dma(SY, "dq", wfb[:, jg % 2, :, :], src,
                                 waits=[("st", prog.cur("st"))])
                        for j in range(8):
                            lm = op(TE, sem="st",
                                    waits=[("dq", ld), ("sv", prog.cur("sv"))]
                                    if j == 0 else ())(
                                lambda t, j=j, jg=jg, nb2=nb2: t.matmul(
                                    psF[:, nb2 * 512:(nb2 + 1) * 512],
                                    xq2t[:, jg * 8 + j, :], wfb[:, jg % 2, j, :],
                                    start=(jg == 0 and j == 0),
                                    stop=(jg == 3 and j == 7)))
                    op(V, sem="sv", waits=[("st", lm)])(
                        lambda v, nb2=nb2: v.tensor_scalar(
                            ffout[:, nb2 * 512:(nb2 + 1) * 512],
                            psF[:, nb2 * 512:(nb2 + 1) * 512], cd, None, ALU.mult))
                    op(TE, sem=None, waits=[("sv", prog.cur("sv"))])(lambda t: t.nop())
                emit_ln(ffout[:, :], lnb[:, :], "ln4")
                ov = op(V, sem="sv", waits=[("dq", prog.cur("dq"))])(
                    lambda v, s=s: v.tensor_tensor(
                        outb[:, :], xch[:, s, :], lnb[:, :], ALU.add))
                dma(SY, "dq", y_out[s * T + c * CH: s * T + c * CH + CH, :],
                    outb[:, :], waits=[("sv", ov)])

        op(SY, sem=None, waits=[("dq", prog.cur("dq")), ("dg", prog.cur("dg"))])(
            lambda e: e.nop())

        fin = {k: prog.cur(k) for k in ("dq", "dg", "sv", "ss", "st")}

        @block.gpsimd
        def _(eng):
            if USE_BOOT:
                for h in (dq, dg, sv, ss_, st_):
                    eng.sem_clear(h)
            eng.drain().then_inc(sv, 1)
            for fn in prog.ops["gpsimd"]:
                fn(eng)
            if USE_EXIT:
                for k in ("dq", "dg", "sv", "ss", "st"):
                    if fin[k]:
                        eng.wait_ge(SEM[k], fin[k])
                for h in (dq, dg, sv, ss_, st_):
                    eng.sem_clear(h)

        @block.sync
        def _(eng):
            for _i in range(256):
                eng.nop()
            eng.wait_ge(sv, 1)
            for fn in prog.ops["sync"]:
                fn(eng)

        @block.vector
        def _(eng):
            for _i in range(256):
                eng.nop()
            eng.wait_ge(sv, 1)
            for fn in prog.ops["vector"]:
                fn(eng)

        @block.scalar
        def _(eng):
            for _i in range(256):
                eng.nop()
            eng.wait_ge(sv, 1)
            for fn in prog.ops["scalar"]:
                fn(eng)

        @block.tensor
        def _(eng):
            for _i in range(256):
                eng.nop()
            eng.wait_ge(sv, 1)
            for fn in prog.ops["tensor"]:
                fn(eng)

    nc._prog_cnt = dict(prog.cnt)
    return nc


def make_in_maps(inputs, T):
    x = np.ascontiguousarray(np.asarray(inputs["x"], dtype=np.float32))
    per = x.shape[0] // NCORES
    common = {k: v for k, v in inputs.items() if k != "x"}
    in_maps = []
    for c in range(NCORES):
        m = dict(common)
        m["x"] = x[c * per:(c + 1) * per].reshape(per * T, HID)
        in_maps.append(m)
    return in_maps, per


def kernel(L0_attn_hidden_states, L0_attn_Wq, L0_attn_bq, L0_attn_Wk, L0_attn_bk,
           L0_attn_Wv, L0_attn_bv, L0_attn_Wo, L0_attn_bo,
           L0_ffn_w_gate, L0_ffn_w_up, L0_ffn_w_down):
    x = np.asarray(L0_attn_hidden_states, dtype=np.float32)
    B, T, H = x.shape
    nc = build_program(T=T)
    inputs = {
        "x": x,
        "wq": np.ascontiguousarray(np.asarray(L0_attn_Wq, np.float32).T),
        "wk": np.ascontiguousarray(np.asarray(L0_attn_Wk, np.float32).T),
        "wv": np.ascontiguousarray(np.asarray(L0_attn_Wv, np.float32).T),
        "wo": np.ascontiguousarray(np.asarray(L0_attn_Wo, np.float32).T),
        "bq": np.asarray(L0_attn_bq, np.float32), "bk": np.asarray(L0_attn_bk, np.float32),
        "bv": np.asarray(L0_attn_bv, np.float32), "bo": np.asarray(L0_attn_bo, np.float32),
        "wg": np.ascontiguousarray(np.asarray(L0_ffn_w_gate, np.float32).T),
        "wu": np.ascontiguousarray(np.asarray(L0_ffn_w_up, np.float32).T),
        "wd": np.ascontiguousarray(np.asarray(L0_ffn_w_down, np.float32).T),
    }
    in_maps, per = make_in_maps(inputs, T)
    res = run_bass_kernel_spmd(nc, in_maps, core_ids=list(range(NCORES)))
    out = np.concatenate([r["y"].reshape(per, T, H) for r in res.results], axis=0)
    return np.ascontiguousarray(out.astype(np.float32))


# version 17 (final)
# speedup vs baseline: 1.2641x; 1.0055x over previous
import numpy as np
from contextlib import ExitStack
import concourse.bass as bass
import concourse.mybir as mybir
from concourse.bass_utils import run_bass_kernel_spmd

F32 = mybir.dt.float32
BF16 = mybir.dt.bfloat16
ALU = mybir.AluOpType
ACT = mybir.ActivationFunctionType
AX = mybir.AxisListType

USE_DRAIN = True
USE_BOOT = True
USE_EXIT = False
LOCKSTEP = False
RELAX_REGIONS = set()
MAGIC = 12582912.0
NCORES = 8
NSEQ = 4
HID = 1024
HEADS = 16
HDIM = 64
INNER = 4096
CH = 128
HB = 64
BETA = 0.9
EPS_LN = 1e-5
EPS_L2 = 1e-12
EPS_WQ = 1e-5
EPS_AQ = 1e-5


class P:
    def __init__(self):
        self.ops = {"sync": [], "gpsimd": [], "vector": [], "scalar": [], "tensor": []}
        self.cnt = {"sv": 1}
        self.region = "init"

    def emit(self, eng, fn):
        self.ops[eng].append(fn)

    def inc(self, sem, n=1):
        self.cnt[sem] = self.cnt.get(sem, 0) + n
        return self.cnt[sem]

    def cur(self, sem):
        return self.cnt.get(sem, 0)


def build_program(T=2048, dbg=False):
    nc = bass.Bass("TRN2", target_bir_lowering=False, detect_race_conditions=False)
    NCH = T // CH

    x_in = nc.dram_tensor("x", [NSEQ * T, HID], F32, kind="ExternalInput")
    wq_d = nc.dram_tensor("wq", [HID, HID], F32, kind="ExternalInput")
    wk_d = nc.dram_tensor("wk", [HID, HID], F32, kind="ExternalInput")
    wv_d = nc.dram_tensor("wv", [HID, HID], F32, kind="ExternalInput")
    wo_d = nc.dram_tensor("wo", [HID, HID], F32, kind="ExternalInput")
    b_ds = [nc.dram_tensor(n, [HID], F32, kind="ExternalInput")
            for n in ("bq", "bk", "bv", "bo")]
    wg_d = nc.dram_tensor("wg", [HID, INNER], F32, kind="ExternalInput")
    wu_d = nc.dram_tensor("wu", [HID, INNER], F32, kind="ExternalInput")
    wd_d = nc.dram_tensor("wd", [INNER, HID], F32, kind="ExternalInput")
    y_out = nc.dram_tensor("y", [NSEQ * T, HID], F32, kind="ExternalOutput")

    wgq_d = nc.dram_tensor("wgq", [HID, INNER], BF16)
    wuq_d = nc.dram_tensor("wuq", [HID, INNER], BF16)
    wdq_d = nc.dram_tensor("wdq", [INNER, HID], BF16)
    qsc_d = nc.dram_tensor("qsc", [CH, NSEQ * HID], F32)
    ksc_d = nc.dram_tensor("ksc", [CH, NSEQ * HID], F32)
    vsc_d = nc.dram_tensor("vsc", [CH, NSEQ * HID], F32)
    osc_d = nc.dram_tensor("osc", [64, CH * HDIM], F32)
    msc_d = nc.dram_tensor("mscd", [8], F32)
    if dbg:
        dbg_q = nc.dram_tensor("dbg_q", [CH, NSEQ * HID], F32, kind="ExternalOutput")
        dbg_kp = nc.dram_tensor("dbg_kp", [64, HB * HDIM], F32, kind="ExternalOutput")
        dbg_o = nc.dram_tensor("dbg_o", [64, CH * HDIM], F32, kind="ExternalOutput")
        dbg_xa = nc.dram_tensor("dbg_xa", [128, NSEQ * HID], F32, kind="ExternalOutput")
        dbg_in = nc.dram_tensor("dbg_in", [128, INNER], F32, kind="ExternalOutput")
        dbg_ln = nc.dram_tensor("dbg_ln", [128, HID], BF16, kind="ExternalOutput")
        dbg_xt = nc.dram_tensor("dbg_xt", [128, 8 * 128], BF16, kind="ExternalOutput")
        dbg_pq = nc.dram_tensor("dbg_pq", [128, HID], F32, kind="ExternalOutput")
        dbg_ti = nc.dram_tensor("dbg_ti", [128, 72], F32, kind="ExternalOutput")
        dbg_xc = nc.dram_tensor("dbg_xc", [128, NSEQ * HID], F32, kind="ExternalOutput")

    prog = P()

    with ExitStack() as stack:
        brep = stack.enter_context(nc.sbuf_tensor("brep", [128, 4, HID], F32))
        xch = stack.enter_context(nc.sbuf_tensor("xch", [128, NSEQ, HID], F32))
        sqb = stack.enter_context(nc.sbuf_tensor("sq", [128, HID], F32))
        lnb = stack.enter_context(nc.sbuf_tensor("lnb", [128, HID], F32))
        xnb = stack.enter_context(nc.sbuf_tensor("xnb", [128, HID], BF16))
        xt = stack.enter_context(nc.sbuf_tensor("xt", [128, 8, 128], BF16))
        ptok = stack.enter_context(nc.sbuf_tensor("ptok", [128, HID], F32))
        wstr = stack.enter_context(nc.sbuf_tensor("wstr", [128, 2, HID], BF16))
        wfb = stack.enter_context(nc.sbuf_tensor("wfb", [128, 2, 8, 512], BF16))
        innb = stack.enter_context(nc.sbuf_tensor("inner", [128, INNER], F32))
        xq2 = stack.enter_context(nc.sbuf_tensor("xq2", [128, INNER], BF16))
        xq2t = stack.enter_context(nc.sbuf_tensor("xq2t", [128, 32, 128], BF16))
        ffout = stack.enter_context(nc.sbuf_tensor("ffout", [128, HID], F32))
        outb = stack.enter_context(nc.sbuf_tensor("outb", [128, HID], F32))
        ident = stack.enter_context(nc.sbuf_tensor("ident", [128, 128], BF16))
        tiny = stack.enter_context(nc.sbuf_tensor("tiny", [128, 72], F32))
        ones = stack.enter_context(nc.sbuf_tensor("ones", [128, 1], F32))
        mscb = stack.enter_context(nc.sbuf_tensor("msc", [128, 8], F32))
        kP = stack.enter_context(nc.sbuf_tensor("kP", [64, HB, HDIM], F32))
        qP = stack.enter_context(nc.sbuf_tensor("qP", [64, HB, HDIM], F32))
        vP = stack.enter_context(nc.sbuf_tensor("vP", [64, HB, HDIM], F32))
        S = stack.enter_context(nc.sbuf_tensor("S", [64, HDIM, HDIM], F32))
        scr = stack.enter_context(nc.sbuf_tensor("scr", [64, HDIM, HDIM], F32))
        opair = stack.enter_context(nc.sbuf_tensor("opair", [64, HB, HDIM], F32))
        vold = stack.enter_context(nc.sbuf_tensor("vold", [64, HDIM], F32))
        dcl = stack.enter_context(nc.sbuf_tensor("dcl", [64, HDIM], F32))
        mrow = stack.enter_context(nc.sbuf_tensor("mrow", [64, 4], F32))
        psF = stack.enter_context(nc.psum_tensor("psF", [128, 1024], F32))
        psP = stack.enter_context(nc.psum_tensor("psP", [128, 1024], F32))
        psT = stack.enter_context(nc.psum_tensor("psT", [128, 128], BF16))
        psM = stack.enter_context(nc.psum_tensor("psM", [128, 2], F32))
        dq = stack.enter_context(nc.semaphore("dq"))
        dg = stack.enter_context(nc.semaphore("dg"))
        sv = stack.enter_context(nc.semaphore("sv"))
        ss_ = stack.enter_context(nc.semaphore("ss"))
        st_ = stack.enter_context(nc.semaphore("st"))
        block = stack.enter_context(nc.Block())
        SEM = {"dq": dq, "dg": dg, "sv": sv, "ss": ss_, "st": st_}

        ENG_SEM = {"vector": "sv", "scalar": "ss", "tensor": "st",
                   "gpsimd": "sv", "sync": None}

        def op(eng, sem=None, waits=(), n=1):
            def deco(fn):
                locked = LOCKSTEP and prog.region not in RELAX_REGIONS
                if locked and sem is None and ENG_SEM[eng]:
                    sem2 = ENG_SEM[eng]
                else:
                    sem2 = sem
                if locked:
                    waits2 = [(k, prog.cur(k)) for k in ("sv", "ss", "st", "dq", "dg")]
                else:
                    waits2 = list(waits)
                tgt = prog.inc(sem2, n) if sem2 else None
                wl = tuple((s, v) for (s, v) in waits2 if v and v > 0)
                def run(eng_obj, fn=fn, wl=wl, sem=sem2, n=n, eng=eng):
                    for s_name, val in wl:
                        eng_obj.wait_ge(SEM[s_name], val)
                    r = fn(eng_obj)
                    if n == 16:
                        if sem is not None:
                            r.then_inc(SEM[sem], n)
                        return
                    if eng in ("vector", "scalar", "gpsimd"):
                        d = eng_obj.drain()
                        if sem is not None:
                            d.then_inc(SEM[sem], n)
                    elif sem is not None:
                        eng_obj.drain().then_inc(SEM[sem], n)
                prog.emit(eng, run)
                return tgt
            return deco

        def dma(eng, sem, dst, src, waits=()):
            return op(eng, sem=sem, waits=waits, n=16)(
                lambda e, d=dst, s=src: e.dma_start(d, s))

        V = "vector"; SC = "scalar"; TE = "tensor"; SY = "sync"; GP = "gpsimd"

        tslot = {}

        def ts_slot(name):
            if name not in tslot:
                i = len(tslot)
                assert i < 56
                tslot[name] = tiny[:, i:i + 1]
            return tslot[name]

        l2ss = tiny[:, 56:56 + HEADS]

        op(GP, sem=None)(lambda g: g.memset(ident[:, :], 0.0))
        op(GP, sem=None)(lambda g: g.affine_select(
            out=ident[:, :], in_=ident[:, :], compare_op=ALU.not_equal,
            fill=1.0, base=0, pattern=[[-1, 128]], channel_multiplier=1))
        op(GP, sem=None)(lambda g: g.memset(ones[:, :], 1.0))
        op(GP, sem=None)(lambda g: g.memset(mrow[:, 1:3], 1.0))
        op(GP, sem="sv")(lambda g: g.memset(S[:, :, :], 0.0))
        for i, b_d in enumerate(b_ds):
            dma(SY, "dq", brep[:, i, :], bass.AP(b_d, 0, [[0, 128], [1, HID]]))

        innf = innb[:, :]
        xcf = xch[:, :, :].rearrange("p a b -> p (a b)")

        def quant_weight(w_d, wq_dd, nchunk, ncol, idx):
            acc = ts_slot("wacc")
            red = ts_slot("wred")
            op(V, sem=None)(lambda v: v.memset(acc, 0.0))
            for j in range(nchunk):
                ld = dma(SY, "dq", innf[:, :ncol],
                         w_d.rearrange("(c p) o -> p c o", p=128)[:, j, :],
                         waits=[("sv", prog.cur("sv"))])
                op(V, sem=None, waits=[("dq", ld)])(
                    lambda v: v.tensor_reduce(red, innf[:, :ncol], AX.X, ALU.add,
                                              apply_absolute_value=True))
                op(V, sem="sv")(lambda v: v.tensor_tensor(acc, acc, red, ALU.add))
            t_mm = op(TE, sem="st", waits=[("sv", prog.cur("sv"))])(
                lambda t: t.matmul(psM[0:1, 0:1], acc, ones[:, :], start=True, stop=True))
            mm_t = ts_slot("wmm")
            rec_t = ts_slot("wrec")
            op(V, sem=None, waits=[("st", t_mm)])(
                lambda v: v.tensor_scalar(mm_t[0:1, :], psM[0:1, 0:1],
                                          1.0 / (nchunk * 128 * ncol), EPS_WQ,
                                          ALU.mult, ALU.max))
            op(V, sem="sv")(lambda v: v.reciprocal(rec_t[0:1, :], mm_t[0:1, :]))
            dma(GP, "dg", msc_d[2 * idx:2 * idx + 1], mm_t[0:1, :],
                waits=[("sv", prog.cur("sv"))])
            d2 = dma(GP, "dg", msc_d[2 * idx + 1:2 * idx + 2], rec_t[0:1, :])
            dma(GP, "dg", mscb[:, 2 * idx:2 * idx + 1],
                bass.AP(msc_d, 2 * idx, [[0, 128], [1, 1]]), waits=[("dg", d2)])
            d4 = dma(GP, "dg", mscb[:, 2 * idx + 1:2 * idx + 2],
                     bass.AP(msc_d, 2 * idx + 1, [[0, 128], [1, 1]]))
            for j in range(nchunk):
                ld = dma(SY, "dq", innf[:, :ncol],
                         w_d.rearrange("(c p) o -> p c o", p=128)[:, j, :],
                         waits=[("sv", prog.cur("sv")), ("dg", d4)])
                op(V, sem=None, waits=[("dq", ld), ("dg", prog.cur("dg"))])(
                    lambda v: v.tensor_scalar(xcf[:, :ncol], innf[:, :ncol],
                                              mscb[:, 2 * idx + 1:2 * idx + 2], MAGIC,
                                              ALU.mult, ALU.add))
                op(V, sem=None)(lambda v: v.tensor_scalar(
                    xcf[:, :ncol], xcf[:, :ncol], MAGIC, -1.0, ALU.subtract, ALU.max))
                vq = op(V, sem="sv")(lambda v: v.tensor_scalar(
                    xq2[:, :ncol], xcf[:, :ncol], 1.0, None, ALU.min))
                dma(GP, "dg", wq_dd.rearrange("(c p) o -> p c o", p=128)[:, j, :],
                    xq2[:, :ncol], waits=[("sv", vq)])

        prog.region = "quant"
        quant_weight(wg_d, wgq_d, 8, INNER, 0)
        quant_weight(wu_d, wuq_d, 8, INNER, 1)
        quant_weight(wd_d, wdq_d, 32, HID, 2)

        prog.region = "init"
        def emit_ln(src_ap, out_ap, pref):
            sums = ts_slot(pref + "_s"); ssum = ts_slot(pref + "_q")
            m1 = ts_slot(pref + "_m"); nv = ts_slot(pref + "_n")
            ve = ts_slot(pref + "_v"); sd = ts_slot(pref + "_d")
            rs = ts_slot(pref + "_r")
            op(V, sem="sv")(lambda v: v.tensor_reduce(sums, src_ap, AX.X, ALU.add))
            c_sq = op(SC, sem="ss", waits=[("sv", prog.cur("sv"))])(
                lambda s: s.activation(sqb[:, :], src_ap, ACT.Square))
            op(V, sem=None, waits=[("ss", c_sq)])(
                lambda v: v.tensor_reduce(ssum, sqb[:, :], AX.X, ALU.add))
            op(V, sem=None)(lambda v: v.tensor_scalar(m1, sums, 1.0 / HID, None, ALU.mult))
            op(V, sem=None)(lambda v: v.tensor_scalar(ssum, ssum, 1.0 / HID, None, ALU.mult))
            op(V, sem=None)(lambda v: v.scalar_tensor_tensor(
                nv, m1, m1, ssum, ALU.mult, ALU.subtract))
            op(V, sem="sv")(lambda v: v.tensor_scalar(
                ve, nv, -1.0, EPS_LN, ALU.mult, ALU.add))
            c_rt = op(SC, sem="ss", waits=[("sv", prog.cur("sv"))])(
                lambda s: s.activation(sd, ve, ACT.Sqrt))
            op(V, sem=None, waits=[("ss", c_rt)])(lambda v: v.reciprocal(rs, sd))
            return op(V, sem="sv", waits=[("st", prog.cur("st"))])(
                lambda v: v.tensor_scalar(
                    out_ap, src_ap, m1, rs, ALU.subtract, ALU.mult))

        def emit_transposes(src_ap, dst, ntile):
            last = None
            for j in range(ntile):
                tt = op(TE, sem="st", waits=[("sv", prog.cur("sv"))])(
                    lambda t, j=j: t.transpose(
                        psT[:, :], src_ap[:, j * 128:(j + 1) * 128], ident[:, :]))
                last = op(V, sem="sv", waits=[("st", tt)])(
                    lambda v, j=j: v.tensor_copy(dst[:, j, :], psT[:, :]))
            return last

        def emit_proj(w_dram, bias_idx, out_ap):
            last_mm = None
            for j in range(8):
                ld = dma(SY, "dq", innf[:, :HID],
                         w_dram.rearrange("(c p) o -> p c o", p=128)[:, j, :],
                         waits=[("sv", prog.cur("sv"))])
                cv = op(V, sem="sv", waits=[("dq", ld), ("st", prog.cur("st"))])(
                    lambda v, j=j: v.tensor_copy(wstr[:, j % 2, :], innf[:, :HID]))
                for nh in range(2):
                    last_mm = op(TE, sem="st", waits=[("sv", cv)] if nh == 0 else ())(
                        lambda t, j=j, nh=nh: t.matmul(
                            psP[:, nh * 512:(nh + 1) * 512], xt[:, j, :],
                            wstr[:, j % 2, nh * 512:(nh + 1) * 512],
                            start=(j == 0), stop=(j == 7)))
            ev = None
            for nh in range(2):
                ev = op(V, sem="sv",
                        waits=[("st", last_mm), ("dg", prog.cur("dg"))] if nh == 0 else ())(
                    lambda v, nh=nh: v.tensor_tensor(
                        out_ap[:, nh * 512:(nh + 1) * 512],
                        psP[:, nh * 512:(nh + 1) * 512],
                        brep[:, bias_idx, nh * 512:(nh + 1) * 512], ALU.add))
            return ev

        def emit_l2(p_ap):
            c_sq = op(SC, sem="ss", waits=[("sv", prog.cur("sv"))])(
                lambda s: s.activation(sqb[:, :], p_ap, ACT.Square))
            op(V, sem="sv", waits=[("ss", c_sq)])(
                lambda v: v.tensor_reduce(
                    l2ss, sqb[:, :].rearrange("p (h d) -> p h d", h=HEADS),
                    AX.X, ALU.add))
            c_rt = op(SC, sem="ss", waits=[("sv", prog.cur("sv"))])(
                lambda s: s.activation(l2ss, l2ss, ACT.Sqrt))
            op(V, sem=None, waits=[("ss", c_rt)])(
                lambda v: v.tensor_scalar(l2ss, l2ss, EPS_L2, None, ALU.max))
            op(V, sem=None)(lambda v: v.reciprocal(l2ss, l2ss))
            return op(V, sem="sv")(lambda v: v.tensor_tensor(
                p_ap.rearrange("p (h d) -> p h d", h=HEADS),
                p_ap.rearrange("p (h d) -> p h d", h=HEADS),
                l2ss.unsqueeze(2).broadcast_to([128, HEADS, HDIM]), ALU.mult))

        def emit_actquant(src_ap, ncols, out_bf, sxr_slot, pref):
            amax = ts_slot(pref + "_a"); sx = ts_slot(pref + "_x")
            op(V, sem=None)(lambda v: v.tensor_reduce(
                amax, src_ap, AX.X, ALU.max, apply_absolute_value=True))
            op(V, sem=None)(lambda v: v.tensor_scalar(amax, amax, EPS_AQ, None, ALU.max))
            op(V, sem=None)(lambda v: v.reciprocal(sx, amax))
            op(V, sem=None)(lambda v: v.tensor_scalar(sx, sx, 127.0, None, ALU.mult))
            op(V, sem=None)(lambda v: v.tensor_scalar(
                sxr_slot, amax, 1.0 / 127.0, None, ALU.mult))
            last = None
            first = True
            for qtr in range(0, ncols, HID):
                sl = slice(qtr, qtr + HID)
                op(V, sem=None,
                   waits=[("st", prog.cur("st"))] if first else ())(
                    lambda v, sl=sl: v.tensor_scalar(
                        sqb[:, :], src_ap[:, sl], sx, MAGIC, ALU.mult, ALU.add))
                first = False
                op(V, sem=None)(lambda v: v.tensor_scalar(
                    sqb[:, :], sqb[:, :], MAGIC, -128.0, ALU.subtract, ALU.max))
                last = op(V, sem="sv")(lambda v, sl=sl: v.tensor_scalar(
                    out_bf[:, sl], sqb[:, :], 127.0, None, ALU.min))
            return last

        for c in range(NCH):
            prog.region = "xload"
            for s in range(NSEQ):
                dma(SY, "dq", xch[:, s, :],
                    x_in[s * T + c * CH: s * T + c * CH + CH, :],
                    waits=[("sv", prog.cur("sv"))] if s == 0 else ())

            xw = prog.cur("dq")
            prog.region = "proj"
            for s in range(NSEQ):
                if s == 0:
                    op(V, sem="sv", waits=[("dq", xw)])(lambda v: v.nop())
                emit_ln(xch[:, s, :], xnb[:, :], "ln1")
                if dbg and c == 0 and s == 0:
                    dma(GP, "dg", dbg_ti[:, :], tiny[:, :],
                        waits=[("sv", prog.cur("sv"))])
                    dd = dma(GP, "dg", dbg_xc[:, :],
                             xch[:, :, :].rearrange("p a b -> p (a b)"))
                    op(V, sem="sv", waits=[("dg", dd)])(lambda v: v.nop())
                emit_transposes(xnb[:, :], xt, 8)
                if dbg and c == 0 and s == 0:
                    dma(GP, "dg", dbg_ln[:, :], xnb[:, :],
                        waits=[("sv", prog.cur("sv"))])
                    dma(GP, "dg", dbg_xt[:, :],
                        xt[:, :, :].rearrange("p a b -> p (a b)"))
                for (w_dram, bi, scdst, do_l2) in (
                        (wq_d, 0, qsc_d, True), (wk_d, 1, ksc_d, True),
                        (wv_d, 2, vsc_d, False)):
                    ev = emit_proj(w_dram, bi, ptok)
                    if dbg and c == 0 and s == 0 and bi == 0:
                        dma(GP, "dg", dbg_pq[:, :], ptok[:, :],
                            waits=[("sv", prog.cur("sv"))])
                    if do_l2:
                        ev = emit_l2(ptok[:, :])
                    dma(GP, "dg", scdst[:, s * HID:(s + 1) * HID], ptok[:, :],
                        waits=[("sv", ev)])

            prog.region = "scan"
            for hb in range(CH // HB):
                def rd_ap(scr_d, hb=hb):
                    return bass.AP(scr_d, hb * HB * (NSEQ * HID),
                                   [[64, 64], [NSEQ * HID, HB], [1, 64]])
                dma(GP, "dg", kP[:, :, :], rd_ap(ksc_d),
                    waits=[("sv", prog.cur("sv"))])
                dma(GP, "dg", qP[:, :, :], rd_ap(qsc_d))
                r3 = dma(GP, "dg", vP[:, :, :], rd_ap(vsc_d))
                if dbg and c == 0 and hb == 0:
                    dma(GP, "dg", dbg_q[:, :], qsc_d[:, :])
                    r3 = dma(GP, "dg", dbg_kp[:, :],
                             kP[:, :, :].rearrange("p a b -> p (a b)"),
                             waits=[("dg", r3)])

                cS = mrow[:, 1:2]; ciS = mrow[:, 2:3]; thS = mrow[:, 3:4]
                for t_ in range(HB):
                    w0 = [("dg", r3)] if t_ == 0 else ()
                    op(V, sem=None, waits=w0)(lambda v, t=t_: v.tensor_tensor(
                        scr[:, :, :], S[:, :, :].transpose([0, 2, 1]),
                        kP[:, t, :].unsqueeze(1).broadcast_to([64, HDIM, HDIM]),
                        ALU.mult))
                    op(V, sem=None)(lambda v: v.tensor_reduce(
                        vold[:, :], scr[:, :, :], AX.X, ALU.add))
                    op(V, sem=None)(lambda v: v.tensor_scalar(
                        vold[:, :], vold[:, :], cS, None, ALU.mult))
                    op(V, sem=None)(lambda v, t=t_: v.scalar_tensor_tensor(
                        dcl[:, :], vold[:, :], 1.0, vP[:, t, :], ALU.mult, ALU.subtract))
                    op(V, sem=None)(lambda v: v.tensor_scalar(
                        dcl[:, :], dcl[:, :], -1.0, 1.0, ALU.max, ALU.min))
                    op(V, sem=None)(lambda v: v.tensor_scalar(
                        dcl[:, :], dcl[:, :], ciS, None, ALU.mult))
                    op(V, sem=None)(lambda v, t=t_: v.scalar_tensor_tensor(
                        scr[:, :, :],
                        kP[:, t, :].unsqueeze(2).broadcast_to([64, HDIM, HDIM]), -BETA,
                        dcl[:, :].unsqueeze(1).broadcast_to([64, HDIM, HDIM]),
                        ALU.mult, ALU.mult))
                    op(V, sem=None)(lambda v: v.tensor_tensor(
                        S[:, :, :], S[:, :, :], scr[:, :, :], ALU.add))
                    op(V, sem=None)(lambda v: v.tensor_reduce(
                        mrow[:, 0:1], S[:, :, :], AX.XY, ALU.max,
                        apply_absolute_value=True))
                    op(V, sem=None)(lambda v: v.tensor_scalar(
                        thS, ciS, 10.0, None, ALU.mult))
                    op(V, sem=None)(lambda v: v.tensor_scalar(
                        mrow[:, 0:1], mrow[:, 0:1], thS, None, ALU.is_gt))
                    op(V, sem=None)(lambda v: v.tensor_scalar(
                        thS, mrow[:, 0:1], -0.1, 1.0, ALU.mult, ALU.add))
                    op(V, sem=None)(lambda v: v.tensor_tensor(cS, cS, thS, ALU.mult))
                    op(V, sem=None)(lambda v: v.tensor_scalar(
                        thS, mrow[:, 0:1], 1.0 / 9.0, 1.0, ALU.mult, ALU.add))
                    op(V, sem=None)(lambda v: v.tensor_tensor(ciS, ciS, thS, ALU.mult))
                    op(V, sem=None)(lambda v, t=t_: v.tensor_tensor(
                        scr[:, :, :], S[:, :, :],
                        qP[:, t, :].unsqueeze(1).broadcast_to([64, HDIM, HDIM]),
                        ALU.mult))
                    op(V, sem=None)(lambda v: v.tensor_reduce(
                        vold[:, :], scr[:, :, :], AX.X, ALU.add))
                    op(V, sem=None)(lambda v: v.tensor_scalar(
                        vold[:, :], vold[:, :], cS, None, ALU.mult))
                    op(V, sem=("sv" if t_ == HB - 1 else None))(
                        lambda v, t=t_: v.tensor_scalar(
                            opair[:, t, :], vold[:, :], -1.0, 1.0, ALU.max, ALU.min))
                op(V, sem=None)(lambda v: v.tensor_scalar(
                    S[:, :, :].rearrange("p a b -> p (a b)"),
                    S[:, :, :].rearrange("p a b -> p (a b)"),
                    cS, None, ALU.mult))
                op(V, sem=None)(lambda v: v.memset(mrow[:, 1:3], 1.0))
                dma(GP, "dg",
                    bass.AP(osc_d, hb * HB * HDIM,
                            [[CH * HDIM, 64], [HDIM, HB], [1, 64]]),
                    opair[:, :, :], waits=[("sv", prog.cur("sv"))])

            prog.region = "oproj"
            og = prog.cur("dg")
            for s in range(NSEQ):
                r_o = dma(GP, "dg", ptok[:, :],
                          bass.AP(osc_d, s * HEADS * CH * HDIM,
                                  [[HDIM, CH], [CH * HDIM, HEADS], [1, HDIM]]),
                          waits=[("dg", og), ("sv", prog.cur("sv"))] if s == 0
                          else [("sv", prog.cur("sv"))])
                op(V, sem="sv", waits=[("dg", r_o), ("st", prog.cur("st"))])(
                    lambda v: v.tensor_copy(xnb[:, :], ptok[:, :]))
                emit_transposes(xnb[:, :], xt, 8)
                emit_proj(wo_d, 3, ptok)
                emit_ln(ptok[:, :], lnb[:, :], "ln2")
                op(V, sem="sv")(lambda v, s=s: v.tensor_tensor(
                    xch[:, s, :], xch[:, s, :], lnb[:, :], ALU.add))

            if dbg and c == 0:
                dma(GP, "dg", dbg_o[:, :], osc_d[:, :],
                    waits=[("sv", prog.cur("sv"))])
                dma(GP, "dg", dbg_xa[:, :],
                    xch[:, :, :].rearrange("p a b -> p (a b)"),
                    waits=[("sv", prog.cur("sv"))])
            prog.region = "ffn"
            for s in range(NSEQ):
                emit_ln(xch[:, s, :], lnb[:, :], "ln3")
                sxr = ts_slot("aq1_sr")
                emit_actquant(lnb[:, :], HID, xnb, sxr, "aq1")
                emit_transposes(xnb[:, :], xt, 8)
                cg = ts_slot("aq1_cg"); cu = ts_slot("aq1_cu")
                op(V, sem=None)(lambda v: v.tensor_tensor(cg, sxr, mscb[:, 0:1], ALU.mult))
                op(V, sem="sv")(lambda v: v.tensor_tensor(cu, sxr, mscb[:, 2:3], ALU.mult))

                for which, wsrc, comb in (("g", wgq_d, cg), ("u", wuq_d, cu)):
                    for nb in range(8):
                        src = bass.AP(wsrc, nb * 512,
                                      [[INNER, 128], [128 * INNER, 8], [1, 512]])
                        ld = dma(SY, "dq", wfb[:, nb % 2, :, :], src,
                                 waits=[("st", prog.cur("st"))])
                        lm = None
                        for j in range(8):
                            lm = op(TE, sem="st",
                                    waits=[("dq", ld), ("sv", prog.cur("sv")),
                                           ("ss", prog.cur("ss"))]
                                    if j == 0 else ())(
                                lambda t, j=j, nb=nb: t.matmul(
                                    psF[:, (nb % 2) * 512:(nb % 2) * 512 + 512],
                                    xt[:, j, :], wfb[:, nb % 2, j, :],
                                    start=(j == 0), stop=(j == 7)))
                        if which == "g":
                            op(SC, sem="ss", waits=[("st", lm)])(
                                lambda s_, nb=nb, comb=comb: s_.activation(
                                    innb[:, nb * 512:(nb + 1) * 512],
                                    psF[:, (nb % 2) * 512:(nb % 2) * 512 + 512],
                                    ACT.Silu, scale=comb))
                        else:
                            op(V, sem="sv", waits=[("st", lm), ("ss", prog.cur("ss"))])(
                                lambda v, nb=nb, comb=comb: v.scalar_tensor_tensor(
                                    innb[:, nb * 512:(nb + 1) * 512],
                                    psF[:, (nb % 2) * 512:(nb % 2) * 512 + 512],
                                    comb, innb[:, nb * 512:(nb + 1) * 512],
                                    ALU.mult, ALU.mult))
                    if which == "g":
                        op(TE, sem=None, waits=[("ss", prog.cur("ss"))])(lambda t: t.nop())
                    else:
                        op(TE, sem=None, waits=[("sv", prog.cur("sv"))])(lambda t: t.nop())

                sxr2 = ts_slot("aq2_sr")
                if dbg and c == 0 and s == 0:
                    dma(GP, "dg", dbg_in[:, :], innb[:, :],
                        waits=[("sv", prog.cur("sv")), ("ss", prog.cur("ss"))])
                op(V, sem=None, waits=[("ss", prog.cur("ss"))])(lambda v: v.nop())
                emit_actquant(innb[:, :], INNER, xq2, sxr2, "aq2")
                emit_transposes(xq2[:, :], xq2t, 32)
                cd = ts_slot("aq2_cd")
                op(V, sem="sv")(lambda v: v.tensor_tensor(cd, sxr2, mscb[:, 4:5], ALU.mult))
                lm = None
                for nb2 in range(2):
                    for jg in range(4):
                        src = bass.AP(wdq_d, nb2 * 512 + jg * 8 * 128 * HID,
                                      [[HID, 128], [128 * HID, 8], [1, 512]])
                        ld = dma(SY, "dq", wfb[:, jg % 2, :, :], src,
                                 waits=[("st", prog.cur("st"))])
                        for j in range(8):
                            lm = op(TE, sem="st",
                                    waits=[("dq", ld), ("sv", prog.cur("sv"))]
                                    if j == 0 else ())(
                                lambda t, j=j, jg=jg, nb2=nb2: t.matmul(
                                    psF[:, nb2 * 512:(nb2 + 1) * 512],
                                    xq2t[:, jg * 8 + j, :], wfb[:, jg % 2, j, :],
                                    start=(jg == 0 and j == 0),
                                    stop=(jg == 3 and j == 7)))
                    op(V, sem="sv", waits=[("st", lm)])(
                        lambda v, nb2=nb2: v.tensor_scalar(
                            ffout[:, nb2 * 512:(nb2 + 1) * 512],
                            psF[:, nb2 * 512:(nb2 + 1) * 512], cd, None, ALU.mult))
                    op(TE, sem=None, waits=[("sv", prog.cur("sv"))])(lambda t: t.nop())
                emit_ln(ffout[:, :], lnb[:, :], "ln4")
                ov = op(V, sem="sv", waits=[("dq", prog.cur("dq"))])(
                    lambda v, s=s: v.tensor_tensor(
                        outb[:, :], xch[:, s, :], lnb[:, :], ALU.add))
                dma(SY, "dq", y_out[s * T + c * CH: s * T + c * CH + CH, :],
                    outb[:, :], waits=[("sv", ov)])

        op(SY, sem=None, waits=[("dq", prog.cur("dq")), ("dg", prog.cur("dg"))])(
            lambda e: e.nop())

        fin = {k: prog.cur(k) for k in ("dq", "dg", "sv", "ss", "st")}

        @block.gpsimd
        def _(eng):
            if USE_BOOT:
                for h in (dq, dg, sv, ss_, st_):
                    eng.sem_clear(h)
            eng.drain().then_inc(sv, 1)
            for fn in prog.ops["gpsimd"]:
                fn(eng)
            if USE_EXIT:
                for k in ("dq", "dg", "sv", "ss", "st"):
                    if fin[k]:
                        eng.wait_ge(SEM[k], fin[k])
                for h in (dq, dg, sv, ss_, st_):
                    eng.sem_clear(h)

        @block.sync
        def _(eng):
            for _i in range(256):
                eng.nop()
            eng.wait_ge(sv, 1)
            for fn in prog.ops["sync"]:
                fn(eng)

        @block.vector
        def _(eng):
            for _i in range(256):
                eng.nop()
            eng.wait_ge(sv, 1)
            for fn in prog.ops["vector"]:
                fn(eng)

        @block.scalar
        def _(eng):
            for _i in range(256):
                eng.nop()
            eng.wait_ge(sv, 1)
            for fn in prog.ops["scalar"]:
                fn(eng)

        @block.tensor
        def _(eng):
            for _i in range(256):
                eng.nop()
            eng.wait_ge(sv, 1)
            for fn in prog.ops["tensor"]:
                fn(eng)

    nc._prog_cnt = dict(prog.cnt)
    return nc


def make_in_maps(inputs, T):
    x = np.ascontiguousarray(np.asarray(inputs["x"], dtype=np.float32))
    per = x.shape[0] // NCORES
    common = {k: v for k, v in inputs.items() if k != "x"}
    in_maps = []
    for c in range(NCORES):
        m = dict(common)
        m["x"] = x[c * per:(c + 1) * per].reshape(per * T, HID)
        in_maps.append(m)
    return in_maps, per


def kernel(L0_attn_hidden_states, L0_attn_Wq, L0_attn_bq, L0_attn_Wk, L0_attn_bk,
           L0_attn_Wv, L0_attn_bv, L0_attn_Wo, L0_attn_bo,
           L0_ffn_w_gate, L0_ffn_w_up, L0_ffn_w_down):
    x = np.asarray(L0_attn_hidden_states, dtype=np.float32)
    B, T, H = x.shape
    nc = build_program(T=T)
    inputs = {
        "x": x,
        "wq": np.ascontiguousarray(np.asarray(L0_attn_Wq, np.float32).T),
        "wk": np.ascontiguousarray(np.asarray(L0_attn_Wk, np.float32).T),
        "wv": np.ascontiguousarray(np.asarray(L0_attn_Wv, np.float32).T),
        "wo": np.ascontiguousarray(np.asarray(L0_attn_Wo, np.float32).T),
        "bq": np.asarray(L0_attn_bq, np.float32), "bk": np.asarray(L0_attn_bk, np.float32),
        "bv": np.asarray(L0_attn_bv, np.float32), "bo": np.asarray(L0_attn_bo, np.float32),
        "wg": np.ascontiguousarray(np.asarray(L0_ffn_w_gate, np.float32).T),
        "wu": np.ascontiguousarray(np.asarray(L0_ffn_w_up, np.float32).T),
        "wd": np.ascontiguousarray(np.asarray(L0_ffn_w_down, np.float32).T),
    }
    in_maps, per = make_in_maps(inputs, T)
    res = run_bass_kernel_spmd(nc, in_maps, core_ids=list(range(NCORES)))
    out = np.concatenate([r["y"].reshape(per, T, H) for r in res.results], axis=0)
    return np.ascontiguousarray(out.astype(np.float32))
